# version 02
# baseline (speedup 1.0000x reference)
import numpy as np
import ml_dtypes
import concourse.bass as bass
import concourse.mybir as mybir
from concourse.bass_utils import run_bass_kernel_spmd

F32 = mybir.dt.float32
BF16 = mybir.dt.bfloat16
AF = mybir.ActivationFunctionType
ALU = mybir.AluOpType
AX = mybir.AxisListType

L = 2048
D = 2048
NT = L // 128
NKD = D // 128
NH = 32
DI = 1408
SHI = 2816
DFF = 8192
NE = 8
NC_XBC = 24
EPS = 1e-5
MASKNEG = -1.0e9

DEBUG = False


class Em:
    ENG = ("tensor", "vector", "scalar", "gpsimd", "sync")

    def __init__(self, cnt, start_tok=None):
        self.cnt = cnt
        self.stream = {e: [] for e in self.ENG}
        self.start_tok = start_tok or {}

    def op(self, eng, fn, after=None, inc=None, amt=1):
        waits = {}
        if not self.stream[eng]:
            for s, v in self.start_tok.items():
                waits[s] = max(waits.get(s, 0), v)
        for tok in (after or []):
            if tok:
                for s, v in tok.items():
                    waits[s] = max(waits.get(s, 0), v)
        entry = [fn, waits, []]
        self.stream[eng].append(entry)
        tok = {}
        if inc is not None:
            self.cnt[inc] = self.cnt.get(inc, 0) + amt
            entry[2].append((inc, amt))
            tok = {inc: self.cnt[inc]}
        return tok

    def dma(self, eng, fn, after=None, sem=None):
        sem = sem or {"gpsimd": "dg", "sync": "ds", "scalar": "dc"}[eng]
        return self.op(eng, fn, after=after, inc=sem, amt=16)

    FALLBACK = {"tensor": ("pe", 1), "vector": ("dv", 1),
                "scalar": ("ac", 1), "gpsimd": ("dg", 16), "sync": ("ds", 16)}

    def fence(self):
        for e in self.ENG:
            for entry in reversed(self.stream[e]):
                if entry[0] is not None:
                    if not entry[2]:
                        s, a = self.FALLBACK[e]
                        self.cnt[s] = self.cnt.get(s, 0) + a
                        entry[2].append((s, a))
                    break
        return {s: self.cnt.get(s, 0)
                for s in ("pe", "dv", "ac", "dg", "ds", "dc", "dt",
                          "spA", "spB")}

    def emit(self, block, sems):
        for e in self.ENG:
            ops = self.stream[e]
            if not ops:
                continue

            def body(eng, ops=ops):
                for fn, waits, incs in ops:
                    for s, v in waits.items():
                        if v > 0:
                            eng.wait_ge(sems[s], v)
                    if fn is None:
                        continue
                    inst = fn(eng)
                    for s, a in incs:
                        inst.then_inc(sems[s], a)
            getattr(block, e)(body)


def merge(*toks):
    out = {}
    for t in toks:
        if t:
            for s, v in t.items():
                out[s] = max(out.get(s, 0), v)
    return out


def build_program(debug=False):
    nc = bass.Bass("TRN2", target_bir_lowering=False)

    def din(name, shape, dt=F32):
        return nc.dram_tensor(name, shape, dt, kind="ExternalInput").ap()

    def dint(name, shape, dt=F32):
        kind = "ExternalOutput" if debug else "Internal"
        return nc.dram_tensor(name, shape, dt, kind=kind).ap()

    d_x = din("d_x", [L, D])
    d_wz = din("d_wz", [D, D])
    d_wxbc = din("d_wxbc", [D, 3104])
    d_convw = din("d_convw", [NC_XBC, 128, 4])
    d_convb = din("d_convb", [128, NC_XBC])
    d_dtb = din("d_dtb", [32, 1])
    d_alog = din("d_alog", [32, 1])
    d_drow = din("d_drow", [1, 32])
    d_ssmw = din("d_ssmw", [1, D])
    d_inlnw = din("d_inlnw", [1, D])
    d_postlnw = din("d_postlnw", [1, D])
    d_wo = din("d_wo", [D, D])
    d_wg = din("d_wg", [D, DFF])
    d_wu = din("d_wu", [D, DFF])
    d_wd = din("d_wd", [DFF, D])
    d_gw = din("d_gw", [D, NE])
    d_w1 = din("d_w1", [NE, D, DI])
    d_w3 = din("d_w3", [NE, D, DI])
    d_w2 = din("d_w2", [NE, DI, D])
    d_sw1 = din("d_sw1", [D, SHI])
    d_sw3 = din("d_sw3", [D, SHI])
    d_sw2 = din("d_sw2", [SHI, D])
    d_ident = din("d_ident", [128, 128])
    d_maskneg = din("d_maskneg", [128, 512])
    d_ones = din("d_ones", [1, 128])

    d_out = nc.dram_tensor("d_out", [L, D], F32, kind="ExternalOutput").ap()

    d_zsp = dint("d_zsp", [L, D])
    d_xbcc = dint("d_xbcc", [NC_XBC, 128, L], BF16)
    d_casp = dint("d_casp", [32, L])
    d_ysp = dint("d_ysp", [L, D])
    d_h1sp = dint("d_h1sp", [L, D])
    d_msp = dint("d_msp", [DFF // 128, 128, L], BF16)
    d_x2sp = dint("d_x2sp", [L, D])
    d_ymoe = dint("d_ymoe", [L, D])
    d_wobf = dint("d_wobf", [NKD, 128, D], BF16)
    d_xnbf = dint("d_xnbf", [L, D], BF16)
    d_ynbf = dint("d_ynbf", [L, D], BF16)
    d_h1nbf = dint("d_h1nbf", [L, D], BF16)
    d_x2bf = dint("d_x2bf", [L, D], BF16)
    d_slotdbg = (nc.dram_tensor("d_slotdbg", [NKD, 128, L], BF16,
                                kind="ExternalOutput").ap() if debug else None)
    d_wdbf = dint("d_wdbf", [DFF // 128, 128, D], BF16)

    cnt = {"bar": 0}
    sems = {}

    from contextlib import ExitStack
    with ExitStack() as top:
        for s in ("bar", "dg", "ds", "dc", "pe", "dv", "ac", "dt",
                  "spA", "spB"):
            sems[s] = top.enter_context(nc.semaphore("sem_" + s))
            cnt.setdefault(s, 0)

        def sb(name, shape, dt=F32):
            return top.enter_context(nc.sbuf_tensor(name, shape, dt))

        s_ident = sb("s_ident", [128, 128], F32)
        s_identb = sb("s_identb", [128, 128], BF16)
        s_maskneg = sb("s_maskneg", [128, 512], F32)
        s_ones = sb("s_ones", [1, 128], F32)
        s_negA = sb("s_negA", [32, 1], F32)
        s_dtb = sb("s_dtb", [32, 1], F32)
        s_slot = sb("s_slot", [128, NKD, L], BF16)
        s_convw = sb("s_convw", [128, NC_XBC, 4], F32)
        s_convb = sb("s_convb", [128, NC_XBC], F32)
        s_wg7 = sb("s_wgate", [128, NT, 10], F32)
        s_eps = sb("s_eps", [128, 1], F32)
        ps = [top.enter_context(nc.psum_tensor(f"psb{i}", [128, 512], F32))
              for i in range(8)]

        block = top.enter_context(nc.Block())

        scopes = {}
        open_seq = [0]

        def open_sb(name, shape, dt=F32):
            cm = nc.sbuf_tensor(name, shape, dt)
            ap = cm.__enter__()
            scopes[name] = (open_seq[0], cm)
            open_seq[0] += 1
            return ap

        def close_sb(*names):
            for n in sorted(names, key=lambda n: -scopes[n][0]):
                scopes.pop(n)[1].__exit__(None, None, None)

        ph_tok = {}

        def new_phase():
            return Em(cnt, start_tok=dict(ph_tok))

        def finish(em):
            ph_tok.clear()
            ph_tok.update(em.fence())
            em.emit(block, sems)

        em = new_phase()
        ta = em.dma("sync", lambda e: e.dma_start(out=s_negA[:], in_=d_alog))
        em.dma("sync", lambda e: e.dma_start(out=s_ident[:], in_=d_ident))
        em.dma("sync", lambda e: e.dma_start(out=s_maskneg[:], in_=d_maskneg))
        em.dma("sync", lambda e: e.dma_start(out=s_ones[:], in_=d_ones))
        em.dma("sync", lambda e: e.dma_start(out=s_dtb[:], in_=d_dtb))
        em.dma("sync", lambda e: e.dma_start(
            out=s_convw[:], in_=d_convw.rearrange("i p k -> p i k")))
        em.dma("sync", lambda e: e.dma_start(
            out=s_convb[:], in_=d_convb))
        em.dma("gpsimd", lambda e: e.dma_start(out=s_identb[:], in_=d_ident))
        t2 = em.op("scalar", lambda e: e.activation(s_negA[:], s_negA[:], AF.Exp),
                   after=[ta], inc="ac")
        em.op("vector", lambda e: e.memset(s_eps[:], EPS), inc="dv")
        em.op("vector", lambda e: e.tensor_scalar(
            s_negA[:], s_negA[:], -1.0, None, ALU.mult), after=[t2], inc="dv")
        finish(em)

        em = new_phase()
        s_wbin = open_sb("s_wbin", [128, D], F32)
        s_x1 = open_sb("s_x1", [128, 2, D], F32)
        s_col1 = open_sb("s_col1", [128, 4], F32)
        s_xnbf = open_sb("s_xnbf", [128, 2, D], BF16)
        s_sq1 = open_sb("s_sq1", [128, 2, D], F32)
        s_lnrow = open_sb("s_lnrow", [1, D], F32)
        tr = em.dma("sync", lambda e: e.dma_start(out=s_lnrow[:], in_=d_inlnw))
        wb_toks = []
        for j in range(4):
            tm = em.op("tensor", lambda e, j=j: e.matmul(
                ps[j][:, :], s_ones[:, :], s_lnrow[0:1, j * 512:(j + 1) * 512],
                start=True, stop=True), after=[tr], inc="pe")
            wb_toks.append(em.op("vector", lambda e, j=j: e.tensor_copy(
                s_wbin[:, j * 512:(j + 1) * 512], ps[j][:, :]), after=[tm],
                inc="dv"))
        wbin_tok = merge(*wb_toks)

        cons = [{}, {}]
        for tb in range(NT):
            p = tb % 2
            tl = em.dma("sync", lambda e, tb=tb, p=p: e.dma_start(
                out=s_x1[:, p, :], in_=d_x[tb * 128:(tb + 1) * 128, :]),
                after=[cons[p]])
            tq = em.op("scalar", lambda e, p=p: e.activation(
                s_sq1[:, p, :], s_x1[:, p, :], AF.Square,
                accum_out=s_col1[:, 2 * p:2 * p + 1]), after=[tl], inc="ac")
            tq = em.op("scalar", lambda e, p=p: e.activation(
                s_col1[:, 2 * p + 1:2 * p + 2], s_col1[:, 2 * p:2 * p + 1],
                AF.Ln, scale=1.0 / D, bias=s_eps[:, 0:1]), after=[tq], inc="ac")
            tq = em.op("scalar", lambda e, p=p: e.activation(
                s_col1[:, 2 * p:2 * p + 1], s_col1[:, 2 * p + 1:2 * p + 2],
                AF.Exp, scale=-0.5), after=[tq], inc="ac")
            tn = em.op("vector", lambda e, p=p: e.scalar_tensor_tensor(
                s_xnbf[:, p, :], s_x1[:, p, :], s_col1[:, 2 * p:2 * p + 1],
                s_wbin[:], ALU.mult, ALU.mult), after=[tq, wbin_tok, cons[p]],
                inc="dv")
            engq = "sync"
            spsem = "spA"
            tspl = em.dma(engq, lambda e, tb=tb, p=p: e.dma_start(
                out=d_xnbf[tb * 128:(tb + 1) * 128, :], in_=s_xnbf[:, p, :]),
                after=[tn], sem=spsem)
            for j in range(NKD):
                em.dma(engq, lambda e, tb=tb, j=j: e.dma_start(
                    out=s_slot[:, j, tb * 128:(tb + 1) * 128],
                    in_=d_xnbf[tb * 128:(tb + 1) * 128, j * 128:(j + 1) * 128],
                    transpose=True), after=[tspl] if j == 0 else None,
                    sem="dt")
            cons[p] = tspl
        finish(em)
        close_sb("s_wbin", "s_x1", "s_col1", "s_xnbf", "s_sq1", "s_lnrow")

        em = new_phase()
        s_dtf = open_sb("s_dtf", [32, L], F32)
        s_wbblk = open_sb("s_wbblk", [128, 2, NKD, 128], BF16)
        s_ev = open_sb("s_ev", [128, 2, 515], F32)
        s_cv = open_sb("s_cv", [128, 2, 512], F32)
        s_xbbf = open_sb("s_xbbf", [128, 2, 512], BF16)

        wlast = [{}, {}]
        wload = [{}, {}]
        bank_tok = [{}] * 8
        cv_tok = [{}, {}]
        xb_tok = [{}, {}]
        for dc in range(25):
            pw = dc % 2
            w = 128 if dc < 24 else 32
            wload[pw] = em.dma("gpsimd", lambda e, dc=dc, pw=pw, w=w: e.dma_start(
                out=s_wbblk[:, pw, :, 0:w],
                in_=d_wxbc[:, dc * 128:dc * 128 + w].rearrange(
                    "(kc p) c -> p kc c", p=128)), after=[wlast[pw]])
            mm_toks = [None] * 4
            for k in range(NKD):
                for tbq in range(4):
                    bank = pw * 4 + tbq
                    mm_toks[tbq] = em.op("tensor", lambda e, pw=pw, w=w, k=k,
                                         tbq=tbq, bank=bank: e.matmul(
                        ps[bank][0:w, :], s_wbblk[:, pw, k, 0:w],
                        s_slot[:, k, tbq * 512:(tbq + 1) * 512],
                        start=(k == 0), stop=(k == NKD - 1)),
                        after=[wload[pw], bank_tok[bank]] if k == 0 else None,
                        inc="pe" if k == NKD - 1 else None)
            wlast[pw] = merge(*mm_toks)
            for tbq in range(4):
                bank = pw * 4 + tbq
                if dc < 24:
                    pp = tbq % 2
                    t1 = em.op("vector", lambda e, bank=bank, pp=pp: e.tensor_copy(
                        s_ev[:, pp, 3:515], ps[bank][:, :]),
                        after=[mm_toks[tbq]], inc="dv")
                    bank_tok[bank] = t1
                    if tbq == 0:
                        t2 = em.op("vector", lambda e, pp=pp: e.memset(
                            s_ev[:, pp, 0:3], 0.0), after=[t1], inc="dv")
                    else:
                        t2 = em.op("vector", lambda e, pp=pp: e.tensor_copy(
                            s_ev[:, pp, 0:3], s_ev[:, 1 - pp, 512:515]),
                            after=[t1], inc="dv")
                    t3 = em.op("vector", lambda e, dc=dc, pp=pp: e.tensor_scalar(
                        s_cv[:, pp, :], s_ev[:, pp, 0:512],
                        s_convw[:, dc, 0:1], s_convb[:, dc:dc + 1],
                        ALU.mult, ALU.add), after=[t2, cv_tok[pp]], inc="dv")
                    for kk in range(1, 4):
                        t3 = em.op("vector", lambda e, dc=dc, pp=pp, kk=kk:
                                   e.scalar_tensor_tensor(
                            s_cv[:, pp, :], s_ev[:, pp, kk:512 + kk],
                            s_convw[:, dc, kk:kk + 1], s_cv[:, pp, :],
                            ALU.mult, ALU.add), after=[t3], inc="dv")
                    t4 = em.op("scalar", lambda e, pp=pp: e.activation(
                        s_xbbf[:, pp, :], s_cv[:, pp, :], AF.Silu),
                        after=[t3, xb_tok[pp]], inc="ac")
                    cv_tok[pp] = t4
                    xb_tok[pp] = em.dma("sync", lambda e, dc=dc, tbq=tbq, pp=pp:
                                        e.dma_start(
                        out=d_xbcc[dc, :, tbq * 512:(tbq + 1) * 512],
                        in_=s_xbbf[:, pp, :]), after=[t4])
                else:
                    bank_tok[bank] = em.op("vector", lambda e, bank=bank, tbq=tbq:
                                           e.tensor_scalar(
                        s_dtf[:, tbq * 512:(tbq + 1) * 512], ps[bank][0:32, :],
                        s_dtb[:], None, ALU.add), after=[mm_toks[tbq]], inc="dv")
        finish(em)
        close_sb("s_wbblk", "s_ev", "s_cv", "s_xbbf")

        em = new_phase()
        s_dsp = open_sb("s_dsp", [32, L], F32)
        s_can = open_sb("s_can", [32, L], F32)
        s_tmp25 = open_sb("s_tmp25", [32, L], F32)
        s_zero25 = open_sb("s_zero25", [32, 128], F32)
        t0 = em.op("vector", lambda e: e.memset(s_zero25[:], 0.0), inc="dv")
        t1 = em.op("scalar", lambda e: e.activation(
            s_tmp25[:], s_dtf[:], AF.Abs), inc="ac")
        t1 = em.op("scalar", lambda e: e.activation(
            s_tmp25[:], s_tmp25[:], AF.Exp, scale=-1.0), after=[t1], inc="ac")
        t1 = em.op("scalar", lambda e: e.activation(
            s_tmp25[:], s_tmp25[:], AF.Ln, bias=1.0), after=[t1], inc="ac")
        t2 = em.op("scalar", lambda e: e.activation(
            s_dsp[:], s_dtf[:], AF.Relu), after=[t1], inc="ac")
        t3 = em.op("vector", lambda e: e.tensor_add(
            s_dsp[:], s_dsp[:], s_tmp25[:]), after=[t2], inc="dv")
        t4 = em.op("vector", lambda e: e.tensor_scalar(
            s_tmp25[:], s_dsp[:], s_negA[:], None, ALU.mult), after=[t3],
            inc="dv")
        t5 = t4
        for c in range(NT):
            t5 = em.op("vector", lambda e, c=c: e.tensor_tensor_scan(
                s_can[:, c * 128:(c + 1) * 128],
                s_tmp25[:, c * 128:(c + 1) * 128], s_zero25[:],
                0.0, ALU.add, ALU.add), after=[t4, t0], inc="dv")
        em.dma("sync", lambda e: e.dma_start(out=d_casp, in_=s_can[:]),
               after=[t5])
        finish(em)
        close_sb("s_tmp25", "s_zero25")

        em = new_phase()
        s_wzblk = open_sb("s_wzblk", [128, 2, NKD, 512], BF16)
        s_zev = open_sb("s_zev", [128, 4, 512], F32)
        wload = [{}, {}]
        wlast = [{}, {}]
        zb_tok = [{}] * 4
        bank_tok = [{}] * 8
        for zb in range(4):
            pw = zb % 2
            wload[pw] = em.dma("gpsimd", lambda e, zb=zb, pw=pw: e.dma_start(
                out=s_wzblk[:, pw, :, :],
                in_=d_wz[:, zb * 512:(zb + 1) * 512].rearrange(
                    "(kc p) c -> p kc c", p=128)), after=[wlast[pw]])
            last_mm = None
            for tb in range(NT):
                bank = pw * 4 + tb % 4
                mm = None
                for k in range(NKD):
                    mm = em.op("tensor", lambda e, pw=pw, tb=tb, k=k, bank=bank:
                               e.matmul(
                        ps[bank][:, :], s_slot[:, k, tb * 128:(tb + 1) * 128],
                        s_wzblk[:, pw, k, :],
                        start=(k == 0), stop=(k == NKD - 1)),
                        after=[wload[pw], bank_tok[bank]] if k == 0 else None,
                        inc="pe" if k == NKD - 1 else None)
                last_mm = mm
                pe_ = tb % 4
                tv = em.op("scalar", lambda e, bank=bank, pe_=pe_: e.activation(
                    s_zev[:, pe_, :], ps[bank][:, :], AF.Silu),
                    after=[mm, zb_tok[pe_]], inc="ac")
                bank_tok[bank] = tv
                zb_tok[pe_] = em.dma("sync", lambda e, zb=zb, tb=tb, pe_=pe_:
                                     e.dma_start(
                    out=d_zsp[tb * 128:(tb + 1) * 128, zb * 512:(zb + 1) * 512],
                    in_=s_zev[:, pe_, :]), after=[tv])
            wlast[pw] = last_mm
        finish(em)
        close_sb("s_wzblk", "s_zev")

        em = new_phase()
        s_xbcc = open_sb("s_xbcc", [128, 2, NC_XBC, 128], BF16)
        s_car = open_sb("s_car", [1, 2, 4096], F32)
        s_xT = open_sb("s_xT", [128, 2, 512], BF16)
        s_BT = open_sb("s_BT", [128, 2, 512], BF16)
        s_caT = open_sb("s_caT", [128, 32], F32)
        s_ncaT = open_sb("s_ncaT", [128, 32], F32)
        s_ecaT = open_sb("s_ecaT", [128, 32], F32)
        s_dspT = open_sb("s_dspT", [128, 32], F32)
        s_wdtT = open_sb("s_wdtT", [128, 32], F32)
        s_caQ = open_sb("s_caQ", [1, 64], F32)
        s_Ecol = open_sb("s_Ecol", [128, 32], F32)
        s_Bsw = open_sb("s_Bsw", [128, 4, 128], BF16)
        s_Lx = open_sb("s_Lx", [128, 4, 128], F32)
        s_Gm = open_sb("s_Gm", [128, 2, 128], BF16)
        s_Xds = open_sb("s_Xds", [128, 2, 64], BF16)
        s_o1 = open_sb("s_o1", [128, 2, 64], F32)
        s_yt = open_sb("s_yt", [128, 2, D], F32)
        s_stf = open_sb("s_stf", [128, NH // 2, 64], F32)
        s_stb = open_sb("s_stb", [128, NH // 2, 64], BF16)
        s_Dbc = open_sb("s_Dbc", [128, 32], F32)
        s_DIh = open_sb("s_DIh", [128, NH, 128], BF16)
        s_drow3 = open_sb("s_drow3", [1, 32], F32)
        s_wotr = open_sb("s_wotr", [128, 2, D], BF16)

        wo_ld = [{}, {}]
        for j in range(NKD):
            pj = j % 2
            tl = em.dma("gpsimd", lambda e, j=j, pj=pj: e.dma_start(
                out=s_wotr[:, pj, :], in_=d_wo[j * 128:(j + 1) * 128, :]),
                after=[wo_ld[pj]])
            wo_ld[pj] = em.dma("sync", lambda e, j=j, pj=pj: e.dma_start(
                out=d_wobf[j, :, :], in_=s_wotr[:, pj, :]), after=[tl])
        wobf_done = merge(*wo_ld)

        tdr = em.dma("sync", lambda e: e.dma_start(out=s_drow3[:], in_=d_drow))
        tdm = em.op("tensor", lambda e: e.matmul(
            ps[7][:, 0:32], s_ones[:], s_drow3[0:1, 0:32], start=True, stop=True),
            after=[tdr], inc="pe")
        tdv = em.op("vector", lambda e: e.tensor_copy(s_Dbc[:], ps[7][:, 0:32]),
                    after=[tdm], inc="dv")
        dih_toks = []
        for h in range(NH):
            dih_toks.append(em.op("vector", lambda e, h=h: e.tensor_scalar(
                s_DIh[:, h, :], s_identb[:], s_Dbc[:, h:h + 1], None, ALU.mult),
                after=[tdv], inc="dv"))
        dih_tok = merge(*dih_toks)

        chunk_tails = [{} for _ in range(NT + 2)]
        y_sp_tok = [{}, {}]
        st_upd_tok = [{} for _ in range(NH)]
        st_bf_tok = [{} for _ in range(NH)]
        smm_tok = [{}, {}]
        ymm_tok = [{}, {}]
        lx_cons = [{}] * 4
        for c in range(NT):
            pc = c % 2
            prev_tail = chunk_tails[c + 1]
            prev2_tail = chunk_tails[c]
            t_x = em.dma("gpsimd", lambda e, c=c, pc=pc: e.dma_start(
                out=s_xbcc[:, pc, :, :],
                in_=d_xbcc[:, :, c * 128:(c + 1) * 128].rearrange(
                    "i p t -> p i t")), after=[prev2_tail])
            t_cr = em.dma("gpsimd", lambda e, c=c, pc=pc: e.dma_start(
                out=s_car[0:1, pc, :].rearrange("p (h t) -> p h t", h=32),
                in_=d_casp[None, :, c * 128:(c + 1) * 128]),
                after=[prev2_tail])
            p6b = ps[6][:, :].bitcast(BF16)
            tr_last = None
            for j in range(4):
                tr_last = em.op("tensor", lambda e, pc=pc, j=j, p6b=p6b:
                                e.transpose(
                    p6b[:, j * 128:(j + 1) * 128], s_xbcc[:, pc, j, :],
                    s_identb[:]),
                    after=[t_x, prev_tail] if j == 0 else [t_x], inc="pe")
            for j in range(4):
                tr_last = em.op("tensor", lambda e, pc=pc, j=j, p6b=p6b:
                                e.transpose(
                    p6b[:, 512 + j * 128:512 + (j + 1) * 128],
                    s_xbcc[:, pc, 4 + j, :], s_identb[:]),
                    after=[t_x], inc="pe")
            e_xT = em.op("vector", lambda e, pc=pc, p6b=p6b: e.tensor_copy(
                s_xT[:, pc, :], p6b[:, 0:512]), after=[tr_last, prev_tail],
                inc="dv")
            e_BT = em.op("vector", lambda e, pc=pc, p6b=p6b: e.tensor_copy(
                s_BT[:, pc, :], p6b[:, 512:1024]), after=[tr_last], inc="dv")
            bsw_last = None
            for g8 in range(8):
                ob = (1 - g8 % 2) * 64
                bsw_last = em.op("tensor", lambda e, pc=pc, g8=g8, ob=ob,
                                 p6b=p6b: e.transpose(
                    p6b[ob:ob + 64, (g8 // 2) * 128:(g8 // 2 + 1) * 128],
                    s_BT[:, pc, g8 * 64:(g8 + 1) * 64], s_identb[:]),
                    after=[e_xT, e_BT] if g8 == 0 else None, inc="pe")
            e_Bsw = em.op("vector", lambda e, p6b=p6b: e.tensor_copy(
                s_Bsw[:, :, :].rearrange("p j t -> p (j t)"), p6b[:, 0:512]),
                after=[bsw_last], inc="dv")
            xbt_tok = merge(e_xT, e_BT, e_Bsw)
            tca = em.op("tensor", lambda e, c=c: e.transpose(
                ps[7][:, 0:32], s_can[:, c * 128:(c + 1) * 128],
                s_ident[0:32, 0:32]), after=[xbt_tok, prev_tail], inc="pe")
            tdt = em.op("tensor", lambda e, c=c: e.transpose(
                ps[7][:, 32:64], s_dsp[:, c * 128:(c + 1) * 128],
                s_ident[0:32, 0:32]), after=[tca], inc="pe")
            tcq = em.op("tensor", lambda e, c=c: e.transpose(
                ps[7][0:1, 64:96], s_can[:, c * 128 + 127:c * 128 + 128],
                s_ident[0:32, 0:32]), after=[tdt], inc="pe")
            e_ca = em.op("vector", lambda e: e.tensor_copy(
                s_caT[:], ps[7][:, 0:32]), after=[tcq, prev_tail], inc="dv")
            e_dt = em.op("vector", lambda e: e.tensor_copy(
                s_dspT[:], ps[7][:, 32:64]), after=[tcq], inc="dv")
            e_cq = em.op("vector", lambda e: e.tensor_copy(
                s_caQ[0:1, 0:32], ps[7][0:1, 64:96]), after=[tcq], inc="dv")
            t_nca = em.op("vector", lambda e: e.tensor_scalar(
                s_ncaT[:], s_caT[:], -1.0, None, ALU.mult), after=[e_ca],
                inc="dv")
            t_eca = em.op("scalar", lambda e: e.activation(
                s_ecaT[:], s_caT[:], AF.Exp), after=[e_ca, prev_tail], inc="ac")
            t_er = em.op("scalar", lambda e: e.activation(
                s_caQ[0:1, 32:64], s_caQ[0:1, 0:32], AF.Exp), after=[e_cq],
                inc="ac")
            t_qb = em.op("tensor", lambda e: e.matmul(
                ps[7][:, 96:128], s_ones[:], s_caQ[0:1, 0:32],
                start=True, stop=True), after=[e_cq], inc="pe")
            t_ws = em.op("vector", lambda e: e.tensor_sub(
                s_wdtT[:], ps[7][:, 96:128], s_caT[:]), after=[t_qb], inc="dv")
            t_we = em.op("scalar", lambda e: e.activation(
                s_wdtT[:], s_wdtT[:], AF.Exp), after=[t_ws], inc="ac")
            t_wd = em.op("vector", lambda e: e.tensor_mul(
                s_wdtT[:], s_wdtT[:], s_dspT[:]), after=[t_we, e_dt], inc="dv")
            t_ec = em.op("tensor", lambda e: e.matmul(
                ps[7][:, 128:160], s_ones[0:1, :], s_caQ[0:1, 32:64],
                start=True, stop=True), after=[t_er, t_ws], inc="pe")
            t_ecv = em.op("vector", lambda e: e.tensor_copy(
                s_Ecol[:], ps[7][:, 128:160]), after=[t_ec], inc="dv")

            lexp_tok = [{}] * 4
            gm_tok = [{}, {}]
            yh_tok = [{}, {}]
            o1_tok = [{}, {}]
            xds_tok = [{}, {}]
            l4_tok = {}
            for g in range(8):
                em.op("tensor", lambda e, pc=pc, g=g: e.matmul(
                    ps[0][:, :], s_ones[:], s_car[0:1, pc, g * 512:(g + 1) * 512],
                    start=True, stop=False),
                    after=[t_cr, merge(*lexp_tok), t_eca])
                l4_tok = em.op("tensor", lambda e: e.matmul(
                    ps[0][:, :], s_ident[:], s_maskneg[:],
                    start=False, stop=True), inc="pe")
                for hh in range(4):
                    h = g * 4 + hh
                    kv = h // 4
                    hb = h % 2
                    base = (h % 2) * 64
                    if kv % 2 == h % 2:
                        Bn = s_xbcc[base:base + 64, pc, 4 + kv // 2, :]
                    else:
                        Bn = s_Bsw[base:base + 64, kv // 2, :]
                    Cn = s_xbcc[base:base + 64, pc, 8 + h // 2, :]
                    lexp_tok[hh] = em.op("scalar", lambda e, hh=hh, h=h:
                                         e.activation(
                        s_Lx[:, hh, :], ps[0][:, hh * 128:(hh + 1) * 128],
                        AF.Exp, bias=s_ncaT[:, h:h + 1]),
                        after=[l4_tok, t_nca, lx_cons[hh]], inc="ac")
                    g_mm = em.op("tensor", lambda e, Bn=Bn, Cn=Cn, hb=hb:
                                 e.matmul(
                        ps[1 + hb][:, 0:128], Bn, Cn, start=True, stop=True),
                        after=[t_x, gm_tok[hb]], inc="pe")
                    gm_tok[hb] = em.op("vector", lambda e, hb=hb, hh=hh, h=h:
                                       e.scalar_tensor_tensor(
                        s_Gm[:, hb, :], ps[1 + hb][:, 0:128],
                        s_dspT[:, h:h + 1], s_Lx[:, hh, :], ALU.mult, ALU.mult),
                        after=[g_mm, lexp_tok[hh], ymm_tok[hb]], inc="dv")
                    lx_cons[hh] = gm_tok[hb]
                    xds_tok[hb] = em.op("vector", lambda e, pc=pc, kv=kv, hb=hb,
                                        h=h: e.tensor_scalar(
                        s_Xds[:, hb, :], s_xT[:, pc, kv * 64:kv * 64 + 64],
                        s_wdtT[:, h:h + 1], None, ALU.mult),
                        after=[t_wd, smm_tok[hb]], inc="dv")
                    pY = ps[3 + hb]
                    pYs = ps[5 + hb]
                    em.op("tensor", lambda e, pc=pc, kv=kv, hb=hb, pY=pY:
                          e.matmul(
                        pY[:, 0:64], s_Gm[:, hb, :],
                        s_xT[:, pc, kv * 64:kv * 64 + 64],
                        start=True, stop=False),
                        after=[gm_tok[hb], yh_tok[hb]])
                    ymm_tok[hb] = em.op("tensor",
                                        lambda e, pc=pc, kv=kv, h=h, pY=pY:
                                        e.matmul(
                        pY[:, 0:64], s_DIh[:, h, :],
                        s_xT[:, pc, kv * 64:kv * 64 + 64],
                        start=False, stop=True), after=[dih_tok], inc="pe")
                    if c > 0:
                        ys_mm = em.op("tensor", lambda e, Cn=Cn, h=h, pYs=pYs,
                                      base=base: e.matmul(
                            pYs[:, 0:64], Cn,
                            s_stb[base:base + 64, h // 2, :],
                            start=True, stop=True),
                            after=[st_bf_tok[h], o1_tok[hb], xbt_tok],
                            inc="pe")
                    else:
                        ys_mm = None
                    s_mm_t = em.op("tensor", lambda e, pc=pc, kv=kv, hb=hb,
                                   base=base: e.matmul(
                        ps[7][base:base + 64, 448:512],
                        s_BT[:, pc, kv * 64:kv * 64 + 64],
                        s_Xds[:, hb, :], start=True, stop=True),
                        after=[xds_tok[hb], e_BT, st_upd_tok[(h - 1) % NH]],
                        inc="pe")
                    smm_tok[hb] = s_mm_t
                    if c == 0:
                        st_upd_tok[h] = em.op("vector", lambda e, h=h, base=base:
                                              e.tensor_copy(
                            s_stf[base:base + 64, h // 2, :],
                            ps[7][base:base + 64, 448:512]),
                            after=[s_mm_t], inc="dv")
                    else:
                        st_upd_tok[h] = em.op("vector", lambda e, h=h, base=base:
                                              e.scalar_tensor_tensor(
                            s_stf[base:base + 64, h // 2, :],
                            s_stf[base:base + 64, h // 2, :],
                            s_Ecol[base:base + 64, h:h + 1],
                            ps[7][base:base + 64, 448:512],
                            ALU.mult, ALU.add), after=[s_mm_t, t_ecv], inc="dv")
                    st_bf_tok[h] = em.op("vector", lambda e, h=h, base=base:
                                         e.tensor_copy(
                        s_stb[base:base + 64, h // 2, :],
                        s_stf[base:base + 64, h // 2, :]),
                        after=[st_upd_tok[h], ys_mm], inc="dv")
                    if c > 0:
                        o1_tok[hb] = em.op("vector", lambda e, hb=hb, h=h,
                                           pYs=pYs: e.tensor_scalar(
                            s_o1[:, hb, :], pYs[:, 0:64], s_ecaT[:, h:h + 1],
                            None, ALU.mult), after=[ys_mm, t_eca], inc="dv")
                        yh_tok[hb] = em.op("vector", lambda e, pc=pc, hb=hb, h=h,
                                           pY=pY: e.tensor_add(
                            s_yt[:, pc, h * 64:(h + 1) * 64], pY[:, 0:64],
                            s_o1[:, hb, :]),
                            after=[ymm_tok[hb], o1_tok[hb], y_sp_tok[pc]],
                            inc="dv")
                    else:
                        yh_tok[hb] = em.op("vector", lambda e, pc=pc, hb=hb, h=h,
                                           pY=pY: e.tensor_copy(
                            s_yt[:, pc, h * 64:(h + 1) * 64], pY[:, 0:64]),
                            after=[ymm_tok[hb], y_sp_tok[pc]], inc="dv")
            y_sp_tok[pc] = em.dma("sync", lambda e, c=c, pc=pc: e.dma_start(
                out=d_ysp[c * 128:(c + 1) * 128, :], in_=s_yt[:, pc, :]),
                after=[yh_tok[0], yh_tok[1]])
            chunk_tails[c + 2] = merge(
                yh_tok[0], yh_tok[1], st_bf_tok[NH - 1], smm_tok[0], smm_tok[1],
                ymm_tok[0], ymm_tok[1], merge(*lexp_tok), y_sp_tok[pc])
        finish(em)
        close_sb("s_xbcc", "s_car", "s_xT", "s_BT", "s_caT", "s_ncaT", "s_ecaT",
                 "s_dspT", "s_wdtT", "s_caQ", "s_Ecol", "s_Lx", "s_Gm", "s_Xds",
                 "s_o1", "s_yt", "s_stf", "s_stb", "s_Dbc", "s_DIh", "s_drow3",
                 "s_Bsw",
                 "s_wotr", "s_dsp", "s_can", "s_dtf")

        em = new_phase()
        s_wbs = open_sb("s_wbs", [128, D], F32)
        s_row4 = open_sb("s_row4", [1, D], F32)
        s_z4 = open_sb("s_z4", [128, 2, D], F32)
        s_y4 = open_sb("s_y4", [128, 2, D], F32)
        s_col4 = open_sb("s_col4", [128, 8], F32)
        s_ynbf = open_sb("s_ynbf", [128, 2, D], BF16)

        tr1 = em.dma("sync", lambda e: e.dma_start(out=s_row4[:], in_=d_ssmw))
        wtoks = []
        for j in range(4):
            tm = em.op("tensor", lambda e, j=j: e.matmul(
                ps[j][:, :], s_ones[:], s_row4[0:1, j * 512:(j + 1) * 512],
                start=True, stop=True), after=[tr1], inc="pe")
            wtoks.append(em.op("vector", lambda e, j=j: e.tensor_copy(
                s_wbs[:, j * 512:(j + 1) * 512], ps[j][:, :]), after=[tm],
                inc="dv"))
        wb4_tok = merge(*wtoks)

        zy_cons = [{}, {}]
        yn_cons = [{}, {}]
        for tb in range(NT):
            p = tb % 2
            tz = em.dma("sync", lambda e, tb=tb, p=p: e.dma_start(
                out=s_z4[:, p, :], in_=d_zsp[tb * 128:(tb + 1) * 128, :]),
                after=[zy_cons[p]])
            ty = em.dma("sync", lambda e, tb=tb, p=p: e.dma_start(
                out=s_y4[:, p, :], in_=d_ysp[tb * 128:(tb + 1) * 128, :]),
                after=[zy_cons[p]])
            tg = em.op("vector", lambda e, p=p: e.tensor_mul(
                s_y4[:, p, :], s_y4[:, p, :], s_z4[:, p, :]),
                after=[tz, ty], inc="dv")
            tq = em.op("scalar", lambda e, p=p: e.activation(
                s_z4[:, p, :], s_y4[:, p, :], AF.Square,
                accum_out=s_col4[:, 4 * p:4 * p + 1]), after=[tg], inc="ac")
            tq = em.op("scalar", lambda e, p=p: e.activation(
                s_col4[:, 4 * p + 1:4 * p + 2], s_col4[:, 4 * p:4 * p + 1],
                AF.Ln, scale=1.0 / D, bias=s_eps[:, 0:1]), after=[tq], inc="ac")
            tq = em.op("scalar", lambda e, p=p: e.activation(
                s_col4[:, 4 * p:4 * p + 1], s_col4[:, 4 * p + 1:4 * p + 2],
                AF.Exp, scale=-0.5), after=[tq], inc="ac")
            tn = em.op("vector", lambda e, p=p: e.scalar_tensor_tensor(
                s_ynbf[:, p, :], s_y4[:, p, :], s_col4[:, 4 * p:4 * p + 1],
                s_wbs[:], ALU.mult, ALU.mult),
                after=[tq, wb4_tok, yn_cons[p]], inc="dv")
            zy_cons[p] = merge(tn, tq)
            engq = "sync"
            spsem = "spA"
            tspl = em.dma(engq, lambda e, tb=tb, p=p: e.dma_start(
                out=d_ynbf[tb * 128:(tb + 1) * 128, :], in_=s_ynbf[:, p, :]),
                after=[tn], sem=spsem)
            yn_cons[p] = tspl
            for j in range(NKD):
                em.dma(engq, lambda e, tb=tb, j=j: e.dma_start(
                    out=s_slot[:, j, tb * 128:(tb + 1) * 128],
                    in_=d_ynbf[tb * 128:(tb + 1) * 128,
                               j * 128:(j + 1) * 128],
                    transpose=True), after=[tspl] if j == 0 else None,
                    sem="dt")
        finish(em)
        close_sb("s_wbs", "s_row4", "s_z4", "s_y4", "s_col4",
                 "s_ynbf")

        em = new_phase()
        s_wo4 = open_sb("s_wo4", [128, NKD, D], BF16)
        s_wbp = open_sb("s_wbp", [128, D], F32)
        s_row4b = open_sb("s_row4b", [1, D], F32)
        s_x4 = open_sb("s_x4", [128, 2, D], F32)
        s_h14 = open_sb("s_h14", [128, 2, D], F32)
        s_col4b = open_sb("s_col4b", [128, 8], F32)
        s_h1nbf = open_sb("s_h1nbf", [128, D], BF16)

        woload = em.dma("gpsimd", lambda e: e.dma_start(
            out=s_wo4[:], in_=d_wobf.rearrange("k p d -> p k d")))
        tr2 = em.dma("sync", lambda e: e.dma_start(out=s_row4b[:],
                                                   in_=d_postlnw))
        wtoks = []
        for j in range(4):
            tm = em.op("tensor", lambda e, j=j: e.matmul(
                ps[4 + j][:, :], s_ones[:], s_row4b[0:1, j * 512:(j + 1) * 512],
                start=True, stop=True), after=[tr2], inc="pe")
            wtoks.append(em.op("vector", lambda e, j=j: e.tensor_copy(
                s_wbp[:, j * 512:(j + 1) * 512], ps[4 + j][:, :]), after=[tm],
                inc="dv"))
        wbp_tok = merge(*wtoks)

        x_cons = [{}, {}]
        h14_cons = [{}, {}]
        h1n_cons = {}
        bank_cons = [wbp_tok] * 8
        for tb in range(NT):
            phh = tb % 2
            mm_done = [None] * 4
            for k in range(NKD):
                for ob in range(4):
                    bank = phh * 4 + ob
                    mm_done[ob] = em.op(
                        "tensor", lambda e, tb=tb, k=k, ob=ob, bank=bank:
                        e.matmul(
                            ps[bank][:, :], s_slot[:, k, tb * 128:(tb + 1) * 128],
                            s_wo4[:, k, ob * 512:(ob + 1) * 512],
                            start=(k == 0), stop=(k == NKD - 1)),
                        after=[woload, bank_cons[bank]] if k == 0 else None,
                        inc="pe" if k == NKD - 1 else None)
            tx = em.dma("gpsimd", lambda e, tb=tb, phh=phh: e.dma_start(
                out=s_x4[:, phh, :], in_=d_x[tb * 128:(tb + 1) * 128, :]),
                after=[x_cons[phh]])
            evs = []
            for ob in range(4):
                bank = phh * 4 + ob
                ev = em.op("vector", lambda e, phh=phh, ob=ob, bank=bank:
                           e.tensor_add(
                    s_h14[:, phh, ob * 512:(ob + 1) * 512], ps[bank][:, :],
                    s_x4[:, phh, ob * 512:(ob + 1) * 512]),
                    after=[mm_done[ob], tx, h14_cons[phh]], inc="dv")
                bank_cons[bank] = ev
                evs.append(ev)
            evm = merge(*evs)
            tsp = em.dma("gpsimd", lambda e, tb=tb, phh=phh: e.dma_start(
                out=d_h1sp[tb * 128:(tb + 1) * 128, :], in_=s_h14[:, phh, :]),
                after=[evm])
            tqs = em.op("scalar", lambda e, phh=phh: e.activation(
                s_x4[:, phh, :], s_h14[:, phh, :], AF.Square,
                accum_out=s_col4b[:, 4 * phh:4 * phh + 1]), after=[evm],
                inc="ac")
            x_cons[phh] = tqs
            tq = em.op("scalar", lambda e, phh=phh: e.activation(
                s_col4b[:, 4 * phh + 1:4 * phh + 2],
                s_col4b[:, 4 * phh:4 * phh + 1],
                AF.Ln, scale=1.0 / D, bias=s_eps[:, 0:1]), after=[tqs],
                inc="ac")
            tq = em.op("scalar", lambda e, phh=phh: e.activation(
                s_col4b[:, 4 * phh:4 * phh + 1],
                s_col4b[:, 4 * phh + 1:4 * phh + 2],
                AF.Exp, scale=-0.5), after=[tq], inc="ac")
            tn2 = em.op("vector", lambda e, phh=phh: e.scalar_tensor_tensor(
                s_h1nbf[:], s_h14[:, phh, :],
                s_col4b[:, 4 * phh:4 * phh + 1], s_wbp[:],
                ALU.mult, ALU.mult), after=[tq, h1n_cons], inc="dv")
            engq = "sync"
            spsem = "spA"
            tspl = em.dma(engq, lambda e, tb=tb: e.dma_start(
                out=d_h1nbf[tb * 128:(tb + 1) * 128, :], in_=s_h1nbf[:]),
                after=[tn2], sem=spsem)
            h1n_cons = tspl
            h14_cons[phh] = merge(tsp, tn2)
            for j in range(NKD):
                em.dma(engq, lambda e, tb=tb, j=j: e.dma_start(
                    out=s_slot[:, j, tb * 128:(tb + 1) * 128],
                    in_=d_h1nbf[tb * 128:(tb + 1) * 128,
                                j * 128:(j + 1) * 128],
                    transpose=True),
                    after=[tspl, mm_done[0], mm_done[1], mm_done[2],
                           mm_done[3]] if j == 0 else None, sem="dt")
        finish(em)
        close_sb("s_wo4", "s_wbp", "s_row4b", "s_x4", "s_h14", "s_col4b",
                 "s_h1nbf")

        em = new_phase()
        if debug:
            for j in range(NKD):
                em.dma("sync", lambda e, j=j: e.dma_start(
                    out=d_slotdbg[j, :, :], in_=s_slot[:, j, :]))
        s_wg5 = open_sb("s_wg5", [128, 2, NKD, 128], BF16)
        s_wu5 = open_sb("s_wu5", [128, 2, NKD, 128], BF16)
        s_sil5 = open_sb("s_sil5", [128, 4, 512], F32)
        s_m5 = open_sb("s_m5", [128, 4, 512], BF16)
        s_wdtr = open_sb("s_wdtr", [128, 2, D], BF16)

        wd_ld = [{}, {}]
        for j in range(DFF // 128):
            pj = j % 2
            tl = em.dma("gpsimd", lambda e, j=j, pj=pj: e.dma_start(
                out=s_wdtr[:, pj, :], in_=d_wd[j * 128:(j + 1) * 128, :]),
                after=[wd_ld[pj]])
            wd_ld[pj] = em.dma("sync", lambda e, j=j, pj=pj: e.dma_start(
                out=d_wdbf[j, :, :], in_=s_wdtr[:, pj, :]), after=[tl])
        wdbf_done = merge(*wd_ld)

        wlast5 = [{}, {}]
        ev_cons = [{}] * 4
        bank5 = [{}] * 8
        for dc in range(DFF // 128):
            pw = dc % 2
            twg = em.dma("gpsimd", lambda e, dc=dc, pw=pw: e.dma_start(
                out=s_wg5[:, pw, :, :],
                in_=d_wg[:, dc * 128:(dc + 1) * 128].rearrange(
                    "(kc p) c -> p kc c", p=128)), after=[wlast5[pw]])
            twu = em.dma("gpsimd", lambda e, dc=dc, pw=pw: e.dma_start(
                out=s_wu5[:, pw, :, :],
                in_=d_wu[:, dc * 128:(dc + 1) * 128].rearrange(
                    "(kc p) c -> p kc c", p=128)), after=[twg])
            mmh = [None] * 4
            mmg = [None] * 4
            for k in range(NKD):
                for tbq in range(4):
                    mmh[tbq] = em.op("tensor", lambda e, pw=pw, k=k, tbq=tbq:
                                     e.matmul(
                        ps[tbq][:, :], s_wg5[:, pw, k, :],
                        s_slot[:, k, tbq * 512:(tbq + 1) * 512],
                        start=(k == 0), stop=(k == NKD - 1)),
                        after=[twg, bank5[tbq]] if k == 0 else None,
                        inc="pe" if k == NKD - 1 else None)
            for k in range(NKD):
                for tbq in range(4):
                    mmg[tbq] = em.op("tensor", lambda e, pw=pw, k=k, tbq=tbq:
                                     e.matmul(
                        ps[4 + tbq][:, :], s_wu5[:, pw, k, :],
                        s_slot[:, k, tbq * 512:(tbq + 1) * 512],
                        start=(k == 0), stop=(k == NKD - 1)),
                        after=[twu, bank5[4 + tbq]] if k == 0 else None,
                        inc="pe" if k == NKD - 1 else None)
            wlast5[pw] = merge(mmh[3], mmg[3])
            for tbq in range(4):
                t1 = em.op("scalar", lambda e, tbq=tbq: e.activation(
                    s_sil5[:, tbq, :], ps[tbq][:, :], AF.Silu),
                    after=[mmh[tbq], ev_cons[tbq]], inc="ac")
                bank5[tbq] = t1
                t2 = em.op("vector", lambda e, tbq=tbq: e.tensor_mul(
                    s_m5[:, tbq, :], s_sil5[:, tbq, :], ps[4 + tbq][:, :]),
                    after=[t1, mmg[tbq], ev_cons[tbq]], inc="dv")
                bank5[4 + tbq] = t2
                ev_cons[tbq] = em.dma("sync", lambda e, dc=dc, tbq=tbq:
                                      e.dma_start(
                    out=d_msp[dc, :, tbq * 512:(tbq + 1) * 512],
                    in_=s_m5[:, tbq, :]), after=[t2])
        finish(em)
        close_sb("s_wg5", "s_wu5", "s_sil5", "s_m5", "s_wdtr")

        em = new_phase()
        NKB = DFF // 128
        KBG = 8
        s_wd6 = open_sb("s_wd6", [128, 2, KBG, D], BF16)
        s_m6 = open_sb("s_m6", [128, 2, KBG, 128], BF16)
        s_acc = open_sb("s_acc", [128, 4, D], F32)
        s_h16 = open_sb("s_h16", [128, 2, D], F32)
        s_x6 = open_sb("s_x6", [128, D], F32)
        s_col6 = open_sb("s_col6", [128, 8], F32)
        s_x2bf = open_sb("s_x2bf", [128, 2, D], BF16)

        wd_cons = [wdbf_done, wdbf_done]
        wdmm = [{}, {}]
        m_free = [{}] * 2
        acc_cons = [{}] * 4
        bank6 = [{}] * 8
        tail_cons = [{}, {}]
        x2T_cons = [{}, {}]
        x6_cons = {}
        for tg in range(4):
            for kbg in range(NKB // KBG):
                pw = kbg % 2
                wld = em.dma("sync", lambda e, kbg=kbg, pw=pw: e.dma_start(
                    out=s_wd6[:, pw, :, :],
                    in_=d_wdbf[kbg * KBG:(kbg + 1) * KBG, :, :].rearrange(
                        "kb p d -> p kb d")),
                    after=[wd_cons[pw], wdmm[pw]])
                wd_cons[pw] = wld
                for ti in range(4):
                    t = tg * 4 + ti
                    mp = ti % 2
                    mld = em.dma("gpsimd", lambda e, kbg=kbg, t=t, mp=mp:
                                 e.dma_start(
                        out=s_m6[:, mp, :, :],
                        in_=d_msp[kbg * KBG:(kbg + 1) * KBG, :,
                                  t * 128:(t + 1) * 128].rearrange(
                            "kb p t -> p kb t")),
                        after=[m_free[mp]])
                    last_kb_tok = None
                    for kb in range(KBG):
                        for ob in range(4):
                            bank = (ti % 2) * 4 + ob
                            tokmm = em.op(
                                "tensor",
                                lambda e, mp=mp, pw=pw, kb=kb, ob=ob, bank=bank:
                                e.matmul(
                                    ps[bank][:, :], s_m6[:, mp, kb, :],
                                    s_wd6[:, pw, kb, ob * 512:(ob + 1) * 512],
                                    start=(kb == 0), stop=(kb == KBG - 1)),
                                after=[mld, wld, bank6[bank]]
                                if kb == 0 else None,
                                inc="pe" if (kb == KBG - 1 and ob == 3)
                                else None)
                        last_kb_tok = tokmm
                    m_free[mp] = tokmm
                    wdmm[pw] = last_kb_tok
                    for ob in range(4):
                        bank = (ti % 2) * 4 + ob
                        if kbg == 0:
                            bank6[bank] = em.op(
                                "vector", lambda e, ti=ti, ob=ob, bank=bank:
                                e.tensor_copy(
                                    s_acc[:, ti, ob * 512:(ob + 1) * 512],
                                    ps[bank][:, :]),
                                after=[last_kb_tok, acc_cons[ti]], inc="dv")
                        else:
                            bank6[bank] = em.op(
                                "vector", lambda e, ti=ti, ob=ob, bank=bank:
                                e.tensor_add(
                                    s_acc[:, ti, ob * 512:(ob + 1) * 512],
                                    s_acc[:, ti, ob * 512:(ob + 1) * 512],
                                    ps[bank][:, :]),
                                after=[last_kb_tok], inc="dv")
            for ti in range(4):
                t = tg * 4 + ti
                p = t % 2
                acc_done = merge(*[bank6[(ti % 2) * 4 + ob] for ob in range(4)])
                th1 = em.dma("sync", lambda e, t=t, p=p: e.dma_start(
                    out=s_h16[:, p, :], in_=d_h1sp[t * 128:(t + 1) * 128, :]),
                    after=[tail_cons[p]])
                tx = em.dma("gpsimd", lambda e, t=t: e.dma_start(
                    out=s_x6[:], in_=d_x[t * 128:(t + 1) * 128, :]),
                    after=[x6_cons])
                t1 = em.op("vector", lambda e, ti=ti, p=p: e.tensor_add(
                    s_acc[:, ti, :], s_acc[:, ti, :], s_h16[:, p, :]),
                    after=[acc_done, th1], inc="dv")
                t1 = em.op("vector", lambda e, ti=ti: e.tensor_add(
                    s_acc[:, ti, :], s_acc[:, ti, :], s_x6[:]),
                    after=[t1, tx], inc="dv")
                ts_ = em.op("vector", lambda e, ti=ti, p=p: e.reduce_sum(
                    s_col6[:, 4 * p:4 * p + 1], s_acc[:, ti, :], axis=AX.X),
                    after=[t1], inc="dv")
                tqs = em.op("scalar", lambda e, ti=ti, p=p: e.activation(
                    s_x6[:], s_acc[:, ti, :], AF.Square,
                    accum_out=s_col6[:, 4 * p + 1:4 * p + 2]), after=[t1],
                    inc="ac")
                x6_cons = merge(t1, tqs)
                tm = em.op("vector", lambda e, p=p: e.tensor_scalar(
                    s_col6[:, 4 * p:4 * p + 1], s_col6[:, 4 * p:4 * p + 1],
                    1.0 / D, None, ALU.mult), after=[ts_], inc="dv")
                tv = em.op("vector", lambda e, p=p: e.tensor_scalar(
                    s_col6[:, 4 * p + 1:4 * p + 2],
                    s_col6[:, 4 * p + 1:4 * p + 2],
                    1.0 / D, None, ALU.mult), after=[tqs], inc="dv")
                tv = em.op("vector", lambda e, p=p: e.scalar_tensor_tensor(
                    s_col6[:, 4 * p + 2:4 * p + 3], s_col6[:, 4 * p:4 * p + 1],
                    s_col6[:, 4 * p:4 * p + 1], s_col6[:, 4 * p + 1:4 * p + 2],
                    ALU.mult, ALU.subtract), after=[tm, tv], inc="dv")
                tv = em.op("vector", lambda e, p=p: e.tensor_scalar(
                    s_col6[:, 4 * p + 2:4 * p + 3],
                    s_col6[:, 4 * p + 2:4 * p + 3],
                    -1.0, None, ALU.mult), after=[tv], inc="dv")
                tq2 = em.op("scalar", lambda e, p=p: e.activation(
                    s_col6[:, 4 * p + 3:4 * p + 4],
                    s_col6[:, 4 * p + 2:4 * p + 3],
                    AF.Ln, bias=s_eps[:, 0:1]), after=[tv], inc="ac")
                tq2 = em.op("scalar", lambda e, p=p: e.activation(
                    s_col6[:, 4 * p + 3:4 * p + 4],
                    s_col6[:, 4 * p + 3:4 * p + 4],
                    AF.Exp, scale=-0.5), after=[tq2], inc="ac")
                tx2 = em.op("vector", lambda e, ti=ti, p=p: e.tensor_scalar(
                    s_acc[:, ti, :], s_acc[:, ti, :], s_col6[:, 4 * p:4 * p + 1],
                    s_col6[:, 4 * p + 3:4 * p + 4], ALU.subtract, ALU.mult),
                    after=[tq2], inc="dv")
                tsp = em.dma("sync", lambda e, t=t, ti=ti: e.dma_start(
                    out=d_x2sp[t * 128:(t + 1) * 128, :], in_=s_acc[:, ti, :]),
                    after=[tx2])
                tbf = em.op("vector", lambda e, ti=ti, p=p: e.tensor_copy(
                    s_x2bf[:, p, :], s_acc[:, ti, :]),
                    after=[tx2, x2T_cons[p]], inc="dv")
                engq = "sync"
                spsem = "spA"
                tspl2 = em.dma(engq, lambda e, t=t, p=p: e.dma_start(
                    out=d_x2bf[t * 128:(t + 1) * 128, :], in_=s_x2bf[:, p, :]),
                    after=[tbf], sem=spsem)
                for j in range(NKD):
                    em.dma(engq, lambda e, t=t, j=j: e.dma_start(
                        out=s_slot[:, j, t * 128:(t + 1) * 128],
                        in_=d_x2bf[t * 128:(t + 1) * 128,
                                   j * 128:(j + 1) * 128],
                        transpose=True), after=[tspl2] if j == 0 else None,
                        sem="dt")
                x2T_cons[p] = tspl2
                acc_cons[ti] = merge(tsp, tbf)
                tail_cons[p] = merge(tsp, tbf, tqs)
        finish(em)
        close_sb("s_wd6", "s_m6", "s_acc", "s_h16", "s_x6", "s_col6", "s_x2bf")

        em = new_phase()
        s_gw7s = open_sb("s_gw7s", [128, NKD, 8], BF16)
        s_e7 = open_sb("s_e7", [128, 4, 8], F32)
        s_c7 = open_sb("s_c7", [128, 8], F32)
        tw = em.dma("gpsimd", lambda e: e.dma_start(
            out=s_gw7s[:], in_=d_gw.rearrange("(kc p) c -> p kc c", p=128)))
        em.op("vector", lambda e: e.memset(s_wg7[:, :, 8:10], 1.0), inc="dv")
        bank7 = [{}, {}]
        for tb in range(NT):
            pb = tb % 2
            mm = None
            for k in range(NKD):
                mm = em.op("tensor", lambda e, tb=tb, k=k, pb=pb: e.matmul(
                    ps[pb][:, 0:8], s_slot[:, k, tb * 128:(tb + 1) * 128],
                    s_gw7s[:, k, :], start=(k == 0), stop=(k == NKD - 1)),
                    after=[tw, bank7[pb]] if k == 0 else None,
                    inc="pe" if k == NKD - 1 else None)
            t1 = em.op("vector", lambda e, pb=pb: e.reduce_max(
                s_c7[:, 0:1], ps[pb][:, 0:8], axis=AX.X), after=[mm], inc="dv")
            t1 = em.op("vector", lambda e: e.tensor_scalar(
                s_c7[:, 1:2], s_c7[:, 0:1], -1.0, None, ALU.mult), after=[t1],
                inc="dv")
            t2 = em.op("scalar", lambda e, pb=pb: e.activation(
                s_e7[:, 0, :], ps[pb][:, 0:8], AF.Exp, bias=s_c7[:, 1:2]),
                after=[t1], inc="ac")
            bank7[pb] = t2
            t3 = em.op("vector", lambda e: e.tensor_scalar(
                s_e7[:, 1, :], s_e7[:, 0, :], 1.0, None, ALU.is_ge),
                after=[t2], inc="dv")
            t3 = em.op("vector", lambda e: e.tensor_mul(
                s_e7[:, 2, :], s_e7[:, 0, :], s_e7[:, 1, :]), after=[t3],
                inc="dv")
            t3 = em.op("vector", lambda e: e.tensor_sub(
                s_e7[:, 2, :], s_e7[:, 0, :], s_e7[:, 2, :]), after=[t3],
                inc="dv")
            t3 = em.op("vector", lambda e: e.reduce_max(
                s_c7[:, 2:3], s_e7[:, 2, :], axis=AX.X), after=[t3], inc="dv")
            t4 = em.op("vector", lambda e: e.tensor_scalar(
                s_e7[:, 3, :], s_e7[:, 0, :], s_c7[:, 2:3], None, ALU.is_ge),
                after=[t3], inc="dv")
            t4 = em.op("vector", lambda e: e.tensor_mul(
                s_e7[:, 3, :], s_e7[:, 3, :], s_e7[:, 0, :]), after=[t4],
                inc="dv")
            t5 = em.op("vector", lambda e: e.reduce_sum(
                s_c7[:, 3:4], s_e7[:, 3, :], axis=AX.X), after=[t4], inc="dv")
            t5 = em.op("vector", lambda e: e.reciprocal(
                s_c7[:, 4:5], s_c7[:, 3:4]), after=[t5], inc="dv")
            em.op("vector", lambda e, tb=tb: e.tensor_scalar(
                s_wg7[:, tb, 0:8], s_e7[:, 3, :], s_c7[:, 4:5], None, ALU.mult),
                after=[t5], inc="dv")
        finish(em)
        close_sb("s_gw7s", "s_e7", "s_c7")

        em = new_phase()
        NDC = 11
        s_w13 = open_sb("s_w13", [128, 2, 2, NKD, 128], BF16)
        s_w28 = open_sb("s_w28", [128, NDC, D], BF16)
        s_sil8 = open_sb("s_sil8", [128, 512], F32)
        s_act8 = open_sb("s_act8", [128, NDC, L], BF16)
        s_yb8 = open_sb("s_yb8", [128, 2, D], F32)

        w13_last = [{}, {}]
        w28_last = [{}, {}]
        act_cons = {}
        yb_cons = [{}, {}]
        bank8 = [{}] * 8
        sil_cons = {}

        experts = [("e", i) for i in range(NE)] + [("s", 0), ("s", 1)]
        for ei, (kind, e_id) in enumerate(experts):
            pe8 = ei % 2
            if kind == "e":
                w2src = d_w2[e_id].rearrange("(kb p) d -> p kb d", p=128)
            else:
                w2src = d_sw2[e_id * DI:(e_id + 1) * DI, :].rearrange(
                    "(kb p) d -> p kb d", p=128)
            w28_ld = em.dma("gpsimd", lambda e, w2src=w2src:
                            e.dma_start(out=s_w28[:], in_=w2src),
                            after=[w28_last[0]])
            mmh = [None] * 4
            mmg = [None] * 4
            for dc in range(NDC):
                pw = dc % 2
                if kind == "e":
                    src1 = d_w1[e_id, :, dc * 128:(dc + 1) * 128]
                    src3 = d_w3[e_id, :, dc * 128:(dc + 1) * 128]
                else:
                    off = e_id * DI
                    src1 = d_sw1[:, off + dc * 128:off + (dc + 1) * 128]
                    src3 = d_sw3[:, off + dc * 128:off + (dc + 1) * 128]
                t1 = em.dma("gpsimd", lambda e, src1=src1, pw=pw: e.dma_start(
                    out=s_w13[:, pw, 0, :, :],
                    in_=src1.rearrange("(kc p) c -> p kc c", p=128)),
                    after=[w13_last[pw]])
                t3 = em.dma("gpsimd", lambda e, src3=src3, pw=pw: e.dma_start(
                    out=s_w13[:, pw, 1, :, :],
                    in_=src3.rearrange("(kc p) c -> p kc c", p=128)),
                    after=[t1])
                for k in range(NKD):
                    for tbq in range(4):
                        mmh[tbq] = em.op("tensor", lambda e, pw=pw, k=k, tbq=tbq:
                                         e.matmul(
                            ps[tbq][:, :], s_w13[:, pw, 0, k, :],
                            s_slot[:, k, tbq * 512:(tbq + 1) * 512],
                            start=(k == 0), stop=(k == NKD - 1)),
                            after=[t1, bank8[tbq]] if k == 0 else None,
                            inc="pe" if k == NKD - 1 else None)
                for k in range(NKD):
                    for tbq in range(4):
                        mmg[tbq] = em.op("tensor", lambda e, pw=pw, k=k, tbq=tbq:
                                         e.matmul(
                            ps[4 + tbq][:, :], s_w13[:, pw, 1, k, :],
                            s_slot[:, k, tbq * 512:(tbq + 1) * 512],
                            start=(k == 0), stop=(k == NKD - 1)),
                            after=[t3, bank8[4 + tbq]] if k == 0 else None,
                            inc="pe" if k == NKD - 1 else None)
                w13_last[pw] = merge(mmh[3], mmg[3])
                for tbq in range(4):
                    tsl = em.op("scalar", lambda e, tbq=tbq: e.activation(
                        s_sil8[:, :], ps[tbq][:, :], AF.Silu),
                        after=[mmh[tbq], sil_cons], inc="ac")
                    bank8[tbq] = tsl
                    tmul = em.op("vector", lambda e, dc=dc, tbq=tbq: e.tensor_mul(
                        s_act8[:, dc, tbq * 512:(tbq + 1) * 512], s_sil8[:, :],
                        ps[4 + tbq][:, :]), after=[tsl, mmg[tbq], act_cons],
                        inc="dv")
                    bank8[4 + tbq] = tmul
                    sil_cons = tmul
            act_done = merge(*bank8)
            new_act_cons = []
            w2_last_mm = None
            for t in range(NT):
                pb = t % 2
                mm2 = [None] * 4
                for kb in range(NDC):
                    for ob in range(4):
                        bank = pb * 4 + ob
                        mm2[ob] = em.op(
                            "tensor",
                            lambda e, t=t, kb=kb, ob=ob, pe8=pe8, bank=bank:
                            e.matmul(
                                ps[bank][:, :],
                                s_act8[:, kb, t * 128:(t + 1) * 128],
                                s_w28[:, kb, ob * 512:(ob + 1) * 512],
                                start=(kb == 0), stop=(kb == NDC - 1)),
                            after=[act_done, w28_ld, bank8[bank]]
                            if kb == 0 else None,
                            inc="pe" if kb == NDC - 1 else None)
                w2_last_mm = merge(*mm2)
                ev = []
                for ob in range(4):
                    bank = pb * 4 + ob
                    gcol = 8 + e_id if kind == "s" else e_id
                    tokv = em.op("vector", lambda e, t=t, ob=ob, bank=bank,
                                 gcol=gcol, pb=pb: e.tensor_scalar(
                        s_yb8[:, pb, ob * 512:(ob + 1) * 512], ps[bank][:, :],
                        s_wg7[:, t, gcol:gcol + 1], None, ALU.mult),
                        after=[mm2[ob], yb_cons[pb]], inc="dv")
                    bank8[bank] = tokv
                    ev.append(tokv)
                accop = ALU.bypass if ei == 0 else ALU.add
                yb_cons[pb] = em.dma("gpsimd", lambda e, t=t, pb=pb, accop=accop:
                                     e.dma_start(
                    out=d_ymoe[t * 128:(t + 1) * 128, :], in_=s_yb8[:, pb, :],
                    accum_op=accop), after=[merge(*ev)])
                new_act_cons.append(yb_cons[pb])
            w28_last[0] = w2_last_mm
            act_cons = merge(*new_act_cons)
        finish(em)
        close_sb("s_w13", "s_w28", "s_sil8", "s_act8", "s_yb8")

        em = new_phase()
        s_y9 = open_sb("s_y9", [128, 2, D], F32)
        s_x9 = open_sb("s_x9", [128, 2, D], F32)
        s_sq9 = open_sb("s_sq9", [128, 2, D], F32)
        s_c9 = open_sb("s_c9", [128, 8], F32)
        cons9 = [{}, {}]
        out_toks = []
        for t in range(NT):
            p = t % 2
            ty = em.dma("sync", lambda e, t=t, p=p: e.dma_start(
                out=s_y9[:, p, :], in_=d_ymoe[t * 128:(t + 1) * 128, :]),
                after=[cons9[p]])
            tx = em.dma("gpsimd", lambda e, t=t, p=p: e.dma_start(
                out=s_x9[:, p, :], in_=d_x2sp[t * 128:(t + 1) * 128, :]),
                after=[cons9[p]])
            t1 = em.op("vector", lambda e, p=p: e.tensor_add(
                s_y9[:, p, :], s_y9[:, p, :], s_x9[:, p, :]), after=[ty, tx],
                inc="dv")
            ts_ = em.op("vector", lambda e, p=p: e.reduce_sum(
                s_c9[:, 4 * p:4 * p + 1], s_y9[:, p, :], axis=AX.X),
                after=[t1], inc="dv")
            tqs = em.op("scalar", lambda e, p=p: e.activation(
                s_sq9[:, p, :], s_y9[:, p, :], AF.Square,
                accum_out=s_c9[:, 4 * p + 1:4 * p + 2]), after=[t1], inc="ac")
            tm = em.op("vector", lambda e, p=p: e.tensor_scalar(
                s_c9[:, 4 * p:4 * p + 1], s_c9[:, 4 * p:4 * p + 1], 1.0 / D,
                None, ALU.mult), after=[ts_], inc="dv")
            tv = em.op("vector", lambda e, p=p: e.tensor_scalar(
                s_c9[:, 4 * p + 1:4 * p + 2], s_c9[:, 4 * p + 1:4 * p + 2],
                1.0 / D, None, ALU.mult), after=[tqs], inc="dv")
            tv = em.op("vector", lambda e, p=p: e.scalar_tensor_tensor(
                s_c9[:, 4 * p + 2:4 * p + 3], s_c9[:, 4 * p:4 * p + 1],
                s_c9[:, 4 * p:4 * p + 1], s_c9[:, 4 * p + 1:4 * p + 2],
                ALU.mult, ALU.subtract), after=[tm, tv], inc="dv")
            tv = em.op("vector", lambda e, p=p: e.tensor_scalar(
                s_c9[:, 4 * p + 2:4 * p + 3], s_c9[:, 4 * p + 2:4 * p + 3],
                -1.0, None, ALU.mult), after=[tv], inc="dv")
            tq2 = em.op("scalar", lambda e, p=p: e.activation(
                s_c9[:, 4 * p + 3:4 * p + 4], s_c9[:, 4 * p + 2:4 * p + 3],
                AF.Ln, bias=s_eps[:, 0:1]), after=[tv], inc="ac")
            tq2 = em.op("scalar", lambda e, p=p: e.activation(
                s_c9[:, 4 * p + 3:4 * p + 4], s_c9[:, 4 * p + 3:4 * p + 4],
                AF.Exp, scale=-0.5), after=[tq2], inc="ac")
            to = em.op("vector", lambda e, p=p: e.tensor_scalar(
                s_y9[:, p, :], s_y9[:, p, :], s_c9[:, 4 * p:4 * p + 1],
                s_c9[:, 4 * p + 3:4 * p + 4], ALU.subtract, ALU.mult),
                after=[tq2], inc="dv")
            tst = em.dma("sync", lambda e, t=t, p=p: e.dma_start(
                out=d_out[t * 128:(t + 1) * 128, :], in_=s_y9[:, p, :]),
                after=[to])
            cons9[p] = merge(tst, tqs)
            out_toks.append(tst)
        em.op("gpsimd", None, after=[merge(*out_toks)])
        finish(em)
        close_sb("s_y9", "s_x9", "s_sq9", "s_c9")

    return nc


def host_inputs(inputs, core):
    f = np.ascontiguousarray
    x = np.asarray(inputs["L0_attn_hidden_states"])
    wip = np.asarray(inputs["L0_attn_in_proj_w"])
    maskneg = np.zeros((128, 512), np.float32)
    s = np.arange(128)[:, None]
    t = np.arange(128)[None, :]
    for j in range(4):
        maskneg[:, j * 128:(j + 1) * 128] = np.where(s > t, MASKNEG, 0.0)
    return {
        "d_x": f(x[core].astype(np.float32)),
        "d_wz": f(wip[0:2048, :].T.astype(np.float32)),
        "d_wxbc": f(wip[2048:5152, :].T.astype(np.float32)),
        "d_convw": f(np.asarray(inputs["L0_attn_conv_w"]).reshape(24, 128, 4)
                     .astype(np.float32)),
        "d_convb": f(np.asarray(inputs["L0_attn_conv_b"]).reshape(24, 128)
                     .T.astype(np.float32)),
        "d_dtb": f(np.asarray(inputs["L0_attn_dt_bias"]).reshape(32, 1)
                   .astype(np.float32)),
        "d_alog": f(np.asarray(inputs["L0_attn_A_log"]).reshape(32, 1)
                    .astype(np.float32)),
        "d_drow": f(np.asarray(inputs["L0_attn_D"]).reshape(1, 32)
                    .astype(np.float32)),
        "d_ssmw": f(np.asarray(inputs["L0_attn_ssm_norm_w"]).reshape(1, D)
                    .astype(np.float32)),
        "d_inlnw": f(np.asarray(inputs["L0_attn_input_ln_w"]).reshape(1, D)
                     .astype(np.float32)),
        "d_postlnw": f(np.asarray(inputs["L0_attn_post_ln_w"]).reshape(1, D)
                       .astype(np.float32)),
        "d_wo": f(np.asarray(inputs["L0_attn_out_proj_w"]).T.astype(np.float32)),
        "d_wg": f(np.asarray(inputs["L0_attn_gate_w"]).T.astype(np.float32)),
        "d_wu": f(np.asarray(inputs["L0_attn_up_w"]).T.astype(np.float32)),
        "d_wd": f(np.asarray(inputs["L0_attn_down_w"]).T.astype(np.float32)),
        "d_gw": f(np.asarray(inputs["L0_ffn_gate_w"]).T.astype(np.float32)),
        "d_w1": f(np.asarray(inputs["L0_ffn_w1"]).astype(np.float32)),
        "d_w3": f(np.asarray(inputs["L0_ffn_w3"]).astype(np.float32)),
        "d_w2": f(np.asarray(inputs["L0_ffn_w2"]).astype(np.float32)),
        "d_sw1": f(np.asarray(inputs["L0_ffn_sw1"]).astype(np.float32)),
        "d_sw3": f(np.asarray(inputs["L0_ffn_sw3"]).astype(np.float32)),
        "d_sw2": f(np.asarray(inputs["L0_ffn_sw2"]).astype(np.float32)),
        "d_ident": np.eye(128, dtype=np.float32),
        "d_maskneg": maskneg,
        "d_ones": np.ones((1, 128), np.float32),
    }


def kernel(**inputs):
    nc = build_program(debug=DEBUG)
    in_maps = [host_inputs(inputs, core) for core in range(8)]
    res = run_bass_kernel_spmd(nc, in_maps, core_ids=list(range(8)))
    return np.stack([np.asarray(r["d_out"], np.float32) for r in res.results],
                    axis=0)


# version 12
# speedup vs baseline: 1.0971x; 1.0971x over previous
import numpy as np
import ml_dtypes
import concourse.bass as bass
import concourse.mybir as mybir
from concourse.bass_utils import run_bass_kernel_spmd

F32 = mybir.dt.float32
BF16 = mybir.dt.bfloat16
AF = mybir.ActivationFunctionType
ALU = mybir.AluOpType
AX = mybir.AxisListType

L = 2048
D = 2048
NT = L // 128
NKD = D // 128
NH = 32
DI = 1408
SHI = 2816
DFF = 8192
NE = 8
NC_XBC = 24
EPS = 1e-5
MASKNEG = -1.0e9

DEBUG = False


class Em:
    ENG = ("tensor", "vector", "scalar", "gpsimd", "sync")

    def __init__(self, cnt, start_tok=None):
        self.cnt = cnt
        self.stream = {e: [] for e in self.ENG}
        self.start_tok = start_tok or {}

    def op(self, eng, fn, after=None, inc=None, amt=1):
        waits = {}
        if not self.stream[eng]:
            for s, v in self.start_tok.items():
                waits[s] = max(waits.get(s, 0), v)
        for tok in (after or []):
            if tok:
                for s, v in tok.items():
                    waits[s] = max(waits.get(s, 0), v)
        entry = [fn, waits, []]
        self.stream[eng].append(entry)
        tok = {}
        if inc is not None:
            self.cnt[inc] = self.cnt.get(inc, 0) + amt
            entry[2].append((inc, amt))
            tok = {inc: self.cnt[inc]}
        return tok

    def dma(self, eng, fn, after=None, sem=None):
        sem = sem or {"gpsimd": "dg", "sync": "ds", "scalar": "dc"}[eng]
        return self.op(eng, fn, after=after, inc=sem, amt=16)

    FALLBACK = {"tensor": ("pe", 1), "vector": ("dv", 1),
                "scalar": ("ac", 1), "gpsimd": ("dg", 16), "sync": ("ds", 16)}

    def qtok(self, sem):
        return {sem: self.cnt.get(sem, 0)}

    def fence(self):
        for e in self.ENG:
            for entry in reversed(self.stream[e]):
                if entry[0] is not None:
                    if not entry[2]:
                        s, a = self.FALLBACK[e]
                        self.cnt[s] = self.cnt.get(s, 0) + a
                        entry[2].append((s, a))
                    break
        return {s: self.cnt.get(s, 0)
                for s in ("pe", "dv", "ac", "dg", "ds", "dc", "dt",
                          "spA", "spB")}

    def emit(self, block, sems):
        for e in self.ENG:
            ops = self.stream[e]
            if not ops:
                continue

            def body(eng, ops=ops):
                for fn, waits, incs in ops:
                    for s, v in waits.items():
                        if v > 0:
                            eng.wait_ge(sems[s], v)
                    if fn is None:
                        continue
                    inst = fn(eng)
                    for s, a in incs:
                        inst.then_inc(sems[s], a)
            getattr(block, e)(body)


def merge(*toks):
    out = {}
    for t in toks:
        if t:
            for s, v in t.items():
                out[s] = max(out.get(s, 0), v)
    return out


def build_program(debug=False):
    nc = bass.Bass("TRN2", target_bir_lowering=False)

    def din(name, shape, dt=F32):
        return nc.dram_tensor(name, shape, dt, kind="ExternalInput").ap()

    def dint(name, shape, dt=F32):
        kind = "ExternalOutput" if debug else "Internal"
        return nc.dram_tensor(name, shape, dt, kind=kind).ap()

    d_x = din("d_x", [L, D])
    d_wz = din("d_wz", [D, D])
    d_wxbc = din("d_wxbc", [D, 3104])
    d_convw = din("d_convw", [NC_XBC, 128, 4])
    d_convb = din("d_convb", [128, NC_XBC])
    d_dtb = din("d_dtb", [32, 1])
    d_alog = din("d_alog", [32, 1])
    d_drow = din("d_drow", [1, 32])
    d_ssmw = din("d_ssmw", [1, D])
    d_inlnw = din("d_inlnw", [1, D])
    d_postlnw = din("d_postlnw", [1, D])
    d_wo = din("d_wo", [D, D])
    d_wg = din("d_wg", [D, DFF])
    d_wu = din("d_wu", [D, DFF])
    d_wd = din("d_wd", [DFF, D])
    d_gw = din("d_gw", [D, NE])
    d_w1 = din("d_w1", [NE, D, DI])
    d_w3 = din("d_w3", [NE, D, DI])
    d_w2 = din("d_w2", [NE, DI, D])
    d_sw1 = din("d_sw1", [D, SHI])
    d_sw3 = din("d_sw3", [D, SHI])
    d_sw2 = din("d_sw2", [SHI, D])
    d_ident = din("d_ident", [128, 128])
    d_maskneg = din("d_maskneg", [128, 512])
    d_ones = din("d_ones", [1, 128])

    d_out = nc.dram_tensor("d_out", [L, D], F32, kind="ExternalOutput").ap()

    d_zsp = dint("d_zsp", [L, D])
    d_xbcc = dint("d_xbcc", [NC_XBC, 128, L], BF16)
    d_casp = dint("d_casp", [32, L])
    d_ysp = dint("d_ysp", [L, D])
    d_h1sp = dint("d_h1sp", [L, D])
    d_msp = dint("d_msp", [DFF // 128, 128, L], BF16)
    d_x2sp = dint("d_x2sp", [L, D])
    d_ymoe = dint("d_ymoe", [L, D])
    d_wobf = dint("d_wobf", [NKD, 128, D], BF16)
    d_xnbf = dint("d_xnbf", [L, D], BF16)
    d_ynbf = dint("d_ynbf", [L, D], BF16)
    d_h1nbf = dint("d_h1nbf", [L, D], BF16)
    d_x2bf = dint("d_x2bf", [L, D], BF16)
    d_slotdbg = (nc.dram_tensor("d_slotdbg", [NKD, 128, L], BF16,
                                kind="ExternalOutput").ap() if debug else None)
    d_wdbf = dint("d_wdbf", [DFF // 128, 128, D], BF16)

    cnt = {"bar": 0}
    sems = {}

    from contextlib import ExitStack
    with ExitStack() as top:
        for s in ("bar", "dg", "ds", "dc", "pe", "dv", "ac", "dt",
                  "spA", "spB"):
            sems[s] = top.enter_context(nc.semaphore("sem_" + s))
            cnt.setdefault(s, 0)

        def sb(name, shape, dt=F32):
            return top.enter_context(nc.sbuf_tensor(name, shape, dt))

        s_ident = sb("s_ident", [128, 128], F32)
        s_identb = sb("s_identb", [128, 128], BF16)
        s_maskneg = sb("s_maskneg", [128, 512], F32)
        s_masknegb = sb("s_masknegb", [128, 512], BF16)
        s_ones = sb("s_ones", [1, 128], F32)
        s_negA = sb("s_negA", [32, 1], F32)
        s_dtb = sb("s_dtb", [32, 1], F32)
        s_slot = sb("s_slot", [128, NKD, L], BF16)
        s_convw = sb("s_convw", [128, NC_XBC, 4], F32)
        s_convb = sb("s_convb", [128, NC_XBC], F32)
        s_wg7 = sb("s_wgate", [128, NT, 10], F32)
        s_eps = sb("s_eps", [128, 1], F32)
        ps = [top.enter_context(nc.psum_tensor(f"psb{i}", [128, 512], F32))
              for i in range(8)]

        block = top.enter_context(nc.Block())

        scopes = {}
        open_seq = [0]

        def open_sb(name, shape, dt=F32):
            cm = nc.sbuf_tensor(name, shape, dt)
            ap = cm.__enter__()
            scopes[name] = (open_seq[0], cm)
            open_seq[0] += 1
            return ap

        def close_sb(*names):
            for n in sorted(names, key=lambda n: -scopes[n][0]):
                scopes.pop(n)[1].__exit__(None, None, None)

        ph_tok = {}

        def new_phase():
            return Em(cnt, start_tok=dict(ph_tok))

        def finish(em):
            ph_tok.clear()
            ph_tok.update(em.fence())
            em.emit(block, sems)

        em = new_phase()
        ta = em.dma("sync", lambda e: e.dma_start(out=s_negA[:], in_=d_alog))
        em.dma("sync", lambda e: e.dma_start(out=s_ident[:], in_=d_ident))
        em.dma("sync", lambda e: e.dma_start(out=s_maskneg[:], in_=d_maskneg))
        em.dma("sync", lambda e: e.dma_start(out=s_ones[:], in_=d_ones))
        em.dma("sync", lambda e: e.dma_start(out=s_dtb[:], in_=d_dtb))
        em.dma("sync", lambda e: e.dma_start(
            out=s_convw[:], in_=d_convw.rearrange("i p k -> p i k")))
        em.dma("sync", lambda e: e.dma_start(
            out=s_convb[:], in_=d_convb))
        em.dma("gpsimd", lambda e: e.dma_start(out=s_identb[:], in_=d_ident))
        em.dma("gpsimd", lambda e: e.dma_start(out=s_masknegb[:], in_=d_maskneg))
        t2 = em.op("scalar", lambda e: e.activation(s_negA[:], s_negA[:], AF.Exp),
                   after=[ta], inc="ac")
        em.op("vector", lambda e: e.memset(s_eps[:], EPS), inc="dv")
        em.op("vector", lambda e: e.tensor_scalar(
            s_negA[:], s_negA[:], -1.0, None, ALU.mult), after=[t2], inc="dv")
        finish(em)

        em = new_phase()
        s_wbin = open_sb("s_wbin", [128, D], F32)
        s_x1 = open_sb("s_x1", [128, 2, D], F32)
        s_col1 = open_sb("s_col1", [128, 4], F32)
        s_xnbf = open_sb("s_xnbf", [128, 2, D], BF16)
        s_sq1 = open_sb("s_sq1", [128, 2, D], F32)
        s_lnrow = open_sb("s_lnrow", [1, D], F32)
        tr = em.dma("sync", lambda e: e.dma_start(out=s_lnrow[:], in_=d_inlnw))
        wb_toks = []
        for j in range(4):
            tm = em.op("tensor", lambda e, j=j: e.matmul(
                ps[j][:, :], s_ones[:, :], s_lnrow[0:1, j * 512:(j + 1) * 512],
                start=True, stop=True), after=[tr], inc="pe")
            wb_toks.append(em.op("vector", lambda e, j=j: e.tensor_copy(
                s_wbin[:, j * 512:(j + 1) * 512], ps[j][:, :]), after=[tm],
                inc="dv"))
        wbin_tok = merge(*wb_toks)

        cons = [{}, {}]
        for tb in range(NT):
            p = tb % 2
            tl = em.dma("sync", lambda e, tb=tb, p=p: e.dma_start(
                out=s_x1[:, p, :], in_=d_x[tb * 128:(tb + 1) * 128, :]),
                after=[cons[p]])
            tq = em.op("scalar", lambda e, p=p: e.activation(
                s_sq1[:, p, :], s_x1[:, p, :], AF.Square,
                accum_out=s_col1[:, 2 * p:2 * p + 1]), after=[tl], inc="ac")
            tq = em.op("scalar", lambda e, p=p: e.activation(
                s_col1[:, 2 * p + 1:2 * p + 2], s_col1[:, 2 * p:2 * p + 1],
                AF.Ln, scale=1.0 / D, bias=s_eps[:, 0:1]), after=[tq], inc="ac")
            tq = em.op("scalar", lambda e, p=p: e.activation(
                s_col1[:, 2 * p:2 * p + 1], s_col1[:, 2 * p + 1:2 * p + 2],
                AF.Exp, scale=-0.5), after=[tq], inc="ac")
            tn = em.op("vector", lambda e, p=p: e.scalar_tensor_tensor(
                s_xnbf[:, p, :], s_x1[:, p, :], s_col1[:, 2 * p:2 * p + 1],
                s_wbin[:], ALU.mult, ALU.mult), after=[tq, wbin_tok, cons[p]],
                inc="dv")
            tspl = em.dma("sync", lambda e, tb=tb, p=p: e.dma_start(
                out=d_xnbf[tb * 128:(tb + 1) * 128, :], in_=s_xnbf[:, p, :]),
                after=[tn], sem="spA" if tb % 2 == 0 else "spB")
            cons[p] = tspl
            for j in range(NKD):
                em.dma("sync", lambda e, tb=tb, j=j: e.dma_start(
                    out=s_slot[:, j, tb * 128:(tb + 1) * 128],
                    in_=d_xnbf[tb * 128:(tb + 1) * 128, j * 128:(j + 1) * 128],
                    transpose=True), after=[tspl] if j == 0 else None,
                    sem="dt")
        finish(em)
        close_sb("s_wbin", "s_x1", "s_col1", "s_xnbf", "s_sq1", "s_lnrow")

        em = new_phase()
        s_dtf = open_sb("s_dtf", [32, L], F32)
        s_wbblk = open_sb("s_wbblk", [128, 2, NKD, 128], BF16)
        s_ev = open_sb("s_ev", [128, 2, 515], F32)
        s_cv = open_sb("s_cv", [128, 2, 512], F32)
        s_xbbf = open_sb("s_xbbf", [128, 2, 512], BF16)

        wlast = [{}, {}]
        wload = [{}, {}]
        bank_tok = [{}] * 8
        cv_tok = [{}, {}]
        xb_tok = [{}, {}]
        for dc in range(25):
            pw = dc % 2
            w = 128 if dc < 24 else 32
            wload[pw] = em.dma("gpsimd", lambda e, dc=dc, pw=pw, w=w: e.dma_start(
                out=s_wbblk[:, pw, :, 0:w],
                in_=d_wxbc[:, dc * 128:dc * 128 + w].rearrange(
                    "(kc p) c -> p kc c", p=128)), after=[wlast[pw]])
            mm_toks = [None] * 4
            for k in range(NKD):
                for tbq in range(4):
                    bank = pw * 4 + tbq
                    mm_toks[tbq] = em.op("tensor", lambda e, pw=pw, w=w, k=k,
                                         tbq=tbq, bank=bank: e.matmul(
                        ps[bank][0:w, :], s_wbblk[:, pw, k, 0:w],
                        s_slot[:, k, tbq * 512:(tbq + 1) * 512],
                        start=(k == 0), stop=(k == NKD - 1)),
                        after=[wload[pw], bank_tok[bank]] if k == 0 else None,
                        inc="pe" if k == NKD - 1 else None)
            wlast[pw] = merge(*mm_toks)
            for tbq in range(4):
                bank = pw * 4 + tbq
                if dc < 24:
                    pp = tbq % 2
                    t1 = em.op("vector", lambda e, bank=bank, pp=pp: e.tensor_copy(
                        s_ev[:, pp, 3:515], ps[bank][:, :]),
                        after=[mm_toks[tbq]], inc="dv")
                    bank_tok[bank] = t1
                    if tbq == 0:
                        t2 = em.op("vector", lambda e, pp=pp: e.memset(
                            s_ev[:, pp, 0:3], 0.0), after=[t1], inc="dv")
                    else:
                        t2 = em.op("vector", lambda e, pp=pp: e.tensor_copy(
                            s_ev[:, pp, 0:3], s_ev[:, 1 - pp, 512:515]),
                            after=[t1], inc="dv")
                    t3 = em.op("vector", lambda e, dc=dc, pp=pp: e.tensor_scalar(
                        s_cv[:, pp, :], s_ev[:, pp, 0:512],
                        s_convw[:, dc, 0:1], s_convb[:, dc:dc + 1],
                        ALU.mult, ALU.add), after=[t2, cv_tok[pp]], inc="dv")
                    for kk in range(1, 4):
                        t3 = em.op("vector", lambda e, dc=dc, pp=pp, kk=kk:
                                   e.scalar_tensor_tensor(
                            s_cv[:, pp, :], s_ev[:, pp, kk:512 + kk],
                            s_convw[:, dc, kk:kk + 1], s_cv[:, pp, :],
                            ALU.mult, ALU.add), after=[t3], inc="dv")
                    t4 = em.op("scalar", lambda e, pp=pp: e.activation(
                        s_xbbf[:, pp, :], s_cv[:, pp, :], AF.Silu),
                        after=[t3, xb_tok[pp]], inc="ac")
                    cv_tok[pp] = t4
                    xb_tok[pp] = em.dma("sync", lambda e, dc=dc, tbq=tbq, pp=pp:
                                        e.dma_start(
                        out=d_xbcc[dc, :, tbq * 512:(tbq + 1) * 512],
                        in_=s_xbbf[:, pp, :]), after=[t4])
                else:
                    bank_tok[bank] = em.op("vector", lambda e, bank=bank, tbq=tbq:
                                           e.tensor_scalar(
                        s_dtf[:, tbq * 512:(tbq + 1) * 512], ps[bank][0:32, :],
                        s_dtb[:], None, ALU.add), after=[mm_toks[tbq]], inc="dv")
        finish(em)
        close_sb("s_wbblk", "s_ev", "s_cv", "s_xbbf")

        em = new_phase()
        s_dsp = open_sb("s_dsp", [32, L], F32)
        s_can = open_sb("s_can", [32, L], F32)
        s_tmp25 = open_sb("s_tmp25", [32, L], F32)
        s_zero25 = open_sb("s_zero25", [32, 128], F32)
        t0 = em.op("vector", lambda e: e.memset(s_zero25[:], 0.0), inc="dv")
        t1 = em.op("scalar", lambda e: e.activation(
            s_tmp25[:], s_dtf[:], AF.Abs), inc="ac")
        t1 = em.op("scalar", lambda e: e.activation(
            s_tmp25[:], s_tmp25[:], AF.Exp, scale=-1.0), after=[t1], inc="ac")
        t1 = em.op("scalar", lambda e: e.activation(
            s_tmp25[:], s_tmp25[:], AF.Ln, bias=1.0), after=[t1], inc="ac")
        t2 = em.op("scalar", lambda e: e.activation(
            s_dsp[:], s_dtf[:], AF.Relu), after=[t1], inc="ac")
        t3 = em.op("vector", lambda e: e.tensor_add(
            s_dsp[:], s_dsp[:], s_tmp25[:]), after=[t2], inc="dv")
        t4 = em.op("vector", lambda e: e.tensor_scalar(
            s_tmp25[:], s_dsp[:], s_negA[:], None, ALU.mult), after=[t3],
            inc="dv")
        t5 = t4
        for c in range(NT):
            t5 = em.op("vector", lambda e, c=c: e.tensor_tensor_scan(
                s_can[:, c * 128:(c + 1) * 128],
                s_tmp25[:, c * 128:(c + 1) * 128], s_zero25[:],
                0.0, ALU.add, ALU.add), after=[t4, t0], inc="dv")
        em.dma("sync", lambda e: e.dma_start(out=d_casp, in_=s_can[:]),
               after=[t5])
        finish(em)
        close_sb("s_tmp25", "s_zero25")

        em = new_phase()
        s_wzblk = open_sb("s_wzblk", [128, 2, NKD, 512], BF16)
        s_zev = open_sb("s_zev", [128, 4, 512], F32)
        wload = [{}, {}]
        wlast = [{}, {}]
        zb_tok = [{}] * 4
        bank_tok = [{}] * 8
        for zb in range(4):
            pw = zb % 2
            wload[pw] = em.dma("gpsimd", lambda e, zb=zb, pw=pw: e.dma_start(
                out=s_wzblk[:, pw, :, :],
                in_=d_wz[:, zb * 512:(zb + 1) * 512].rearrange(
                    "(kc p) c -> p kc c", p=128)), after=[wlast[pw]])
            last_mm = None
            for tb in range(NT):
                bank = pw * 4 + tb % 4
                mm = None
                for k in range(NKD):
                    mm = em.op("tensor", lambda e, pw=pw, tb=tb, k=k, bank=bank:
                               e.matmul(
                        ps[bank][:, :], s_slot[:, k, tb * 128:(tb + 1) * 128],
                        s_wzblk[:, pw, k, :],
                        start=(k == 0), stop=(k == NKD - 1)),
                        after=[wload[pw], bank_tok[bank]] if k == 0 else None,
                        inc="pe" if k == NKD - 1 else None)
                last_mm = mm
                pe_ = tb % 4
                tv = em.op("scalar", lambda e, bank=bank, pe_=pe_: e.activation(
                    s_zev[:, pe_, :], ps[bank][:, :], AF.Silu),
                    after=[mm, zb_tok[pe_]], inc="ac")
                bank_tok[bank] = tv
                zb_tok[pe_] = em.dma("sync", lambda e, zb=zb, tb=tb, pe_=pe_:
                                     e.dma_start(
                    out=d_zsp[tb * 128:(tb + 1) * 128, zb * 512:(zb + 1) * 512],
                    in_=s_zev[:, pe_, :]), after=[tv])
            wlast[pw] = last_mm
        finish(em)
        close_sb("s_wzblk", "s_zev")

        em = new_phase()
        s_xbcc = open_sb("s_xbcc", [128, 2, NC_XBC, 128], BF16)
        s_car = open_sb("s_car", [1, 4200], F32)
        s_xT = open_sb("s_xT", [128, 2, 512], BF16)
        s_BT = open_sb("s_BT", [128, 2, 512], BF16)
        s_caT = open_sb("s_caT", [128, 32], F32)
        s_ncaT = open_sb("s_ncaT", [128, 32], F32)
        s_ecaT = open_sb("s_ecaT", [128, 32], F32)
        s_dspT = open_sb("s_dspT", [128, 32], F32)
        s_wdtT = open_sb("s_wdtT", [128, 32], F32)
        s_caQ = open_sb("s_caQ", [1, 64], F32)
        s_Ecol = open_sb("s_Ecol", [128, 32], F32)
        s_Bsw = open_sb("s_Bsw", [128, 4, 128], BF16)
        s_Lx = open_sb("s_Lx", [128, 4, 128], F32)
        s_Gm = open_sb("s_Gm", [128, 2, 128], BF16)
        s_Xds = open_sb("s_Xds", [128, 2, 64], BF16)
        s_o1 = open_sb("s_o1", [128, 2, 64], F32)
        s_yt = open_sb("s_yt", [128, 2, D], F32)
        s_stf = open_sb("s_stf", [128, NH // 2, 64], F32)
        s_stb = open_sb("s_stb", [128, NH // 2, 64], BF16)
        s_zs3 = open_sb("s_zs3", [128, 2, D], F32)
        s_ynbf3 = open_sb("s_ynbf3", [128, D], BF16)
        s_wbs3 = open_sb("s_wbs3", [128, D], F32)
        s_col3 = open_sb("s_col3", [128, 4], F32)
        s_Dbc = open_sb("s_Dbc", [128, 32], F32)
        s_DIh = open_sb("s_DIh", [128, NH, 128], BF16)
        s_drow3 = open_sb("s_drow3", [1, 32], F32)
        s_wotr = open_sb("s_wotr", [128, D], BF16)

        wo_ld = {}
        for j in range(NKD):
            tl = em.dma("gpsimd", lambda e, j=j: e.dma_start(
                out=s_wotr[:], in_=d_wo[j * 128:(j + 1) * 128, :]),
                after=[wo_ld])
            wo_ld = em.dma("sync", lambda e, j=j: e.dma_start(
                out=d_wobf[j, :, :], in_=s_wotr[:]), after=[tl])
        wobf_done = wo_ld

        tdr = em.dma("sync", lambda e: e.dma_start(out=s_drow3[:], in_=d_drow))
        tdm = em.op("tensor", lambda e: e.matmul(
            ps[7][:, 0:32], s_ones[:], s_drow3[0:1, 0:32], start=True, stop=True),
            after=[tdr], inc="pe")
        tdv = em.op("vector", lambda e: e.tensor_copy(s_Dbc[:], ps[7][:, 0:32]),
                    after=[tdm], inc="dv")
        dih_toks = []
        for h in range(NH):
            dih_toks.append(em.op("vector", lambda e, h=h: e.tensor_scalar(
                s_DIh[:, h, :], s_identb[:], s_Dbc[:, h:h + 1], None, ALU.mult),
                after=[tdv], inc="dv"))
        dih_tok = merge(*dih_toks)

        trw = em.dma("sync", lambda e: e.dma_start(
            out=s_car[0:1, 0:D], in_=d_ssmw))
        wb3_toks = []
        wb3_mm = None
        for j in range(4):
            wb3_mm = em.op("tensor", lambda e, j=j: e.matmul(
                ps[j][:, :], s_ones[:], s_car[0:1, j * 512:(j + 1) * 512],
                start=True, stop=True), after=[trw], inc="pe")
            wb3_toks.append(em.op("vector", lambda e, j=j: e.tensor_copy(
                s_wbs3[:, j * 512:(j + 1) * 512], ps[j][:, :]), after=[wb3_mm],
                inc="dv"))
        wb3_tok = merge(*wb3_toks)

        chunk_tails = [{} for _ in range(NT + 2)]
        y_sp_tok = [{}, {}]
        zs_cons = [{}, {}]
        ynbf_cons3 = {}
        car_cons = {}
        st_upd_tok = [{} for _ in range(NH)]
        st_bf_tok = [{} for _ in range(NH)]
        smm_tok = [{}, {}]
        ymm_tok = [{}, {}]
        lx_cons = [{}] * 4
        for c in range(NT):
            pc = c % 2
            prev_tail = chunk_tails[c + 1]
            prev2_tail = chunk_tails[c]
            t_x = em.dma("gpsimd", lambda e, c=c, pc=pc: e.dma_start(
                out=s_xbcc[:, pc, :, :],
                in_=d_xbcc[:, :, c * 128:(c + 1) * 128].rearrange(
                    "i p t -> p i t")), after=[prev2_tail])
            t_cr = em.dma("gpsimd", lambda e, c=c: e.dma_start(
                out=s_car[0:1, 0:4096].rearrange("p (h t) -> p h t", h=32),
                in_=d_casp[None, :, c * 128:(c + 1) * 128]),
                after=[prev_tail, wb3_mm if c == 0 else None, car_cons])
            t_z = em.dma("sync", lambda e, c=c, pc=pc: e.dma_start(
                out=s_zs3[:, pc, :], in_=d_zsp[c * 128:(c + 1) * 128, :]),
                after=[zs_cons[pc]])
            p6b = ps[6][:, :].bitcast(BF16)
            tr_last = None
            for j in range(4):
                tr_last = em.op("tensor", lambda e, pc=pc, j=j, p6b=p6b:
                                e.transpose(
                    p6b[:, j * 128:(j + 1) * 128], s_xbcc[:, pc, j, :],
                    s_identb[:]),
                    after=[t_x, prev_tail] if j == 0 else [t_x], inc="pe")
            for j in range(4):
                tr_last = em.op("tensor", lambda e, pc=pc, j=j, p6b=p6b:
                                e.transpose(
                    p6b[:, 512 + j * 128:512 + (j + 1) * 128],
                    s_xbcc[:, pc, 4 + j, :], s_identb[:]),
                    after=[t_x], inc="pe")
            e_xT = em.op("vector", lambda e, pc=pc, p6b=p6b: e.tensor_copy(
                s_xT[:, pc, :], p6b[:, 0:512]), after=[tr_last, prev_tail],
                inc="dv")
            e_BT = em.op("vector", lambda e, pc=pc, p6b=p6b: e.tensor_copy(
                s_BT[:, pc, :], p6b[:, 512:1024]), after=[tr_last], inc="dv")
            bsw_last = None
            for g8 in range(8):
                ob = (1 - g8 % 2) * 64
                bsw_last = em.op("tensor", lambda e, pc=pc, g8=g8, ob=ob,
                                 p6b=p6b: e.transpose(
                    p6b[ob:ob + 64, (g8 // 2) * 128:(g8 // 2 + 1) * 128],
                    s_BT[:, pc, g8 * 64:(g8 + 1) * 64], s_identb[:]),
                    after=[e_xT, e_BT] if g8 == 0 else None, inc="pe")
            e_Bsw = em.op("vector", lambda e, p6b=p6b: e.tensor_copy(
                s_Bsw[:, :, :].rearrange("p j t -> p (j t)"), p6b[:, 0:512]),
                after=[bsw_last], inc="dv")
            xbt_tok = merge(e_xT, e_BT, e_Bsw)
            tca = em.op("tensor", lambda e, c=c: e.transpose(
                ps[7][:, 0:32], s_can[:, c * 128:(c + 1) * 128],
                s_ident[0:32, 0:32]), after=[xbt_tok, prev_tail], inc="pe")
            tdt = em.op("tensor", lambda e, c=c: e.transpose(
                ps[7][:, 32:64], s_dsp[:, c * 128:(c + 1) * 128],
                s_ident[0:32, 0:32]), after=[tca], inc="pe")
            tcq = em.op("tensor", lambda e, c=c: e.transpose(
                ps[7][0:1, 64:96], s_can[:, c * 128 + 127:c * 128 + 128],
                s_ident[0:32, 0:32]), after=[tdt], inc="pe")
            e_ca = em.op("vector", lambda e: e.tensor_copy(
                s_caT[:], ps[7][:, 0:32]), after=[tcq, prev_tail], inc="dv")
            e_dt = em.op("vector", lambda e: e.tensor_copy(
                s_dspT[:], ps[7][:, 32:64]), after=[tcq], inc="dv")
            e_cq = em.op("vector", lambda e: e.tensor_copy(
                s_caQ[0:1, 0:32], ps[7][0:1, 64:96]), after=[tcq], inc="dv")
            t_nca = em.op("vector", lambda e: e.tensor_scalar(
                s_ncaT[:], s_caT[:], -1.0, None, ALU.mult), after=[e_ca],
                inc="dv")
            t_eca = em.op("scalar", lambda e: e.activation(
                s_ecaT[:], s_caT[:], AF.Exp), after=[e_ca, prev_tail], inc="ac")
            t_er = em.op("scalar", lambda e: e.activation(
                s_caQ[0:1, 32:64], s_caQ[0:1, 0:32], AF.Exp), after=[e_cq],
                inc="ac")
            t_qb = em.op("tensor", lambda e: e.matmul(
                ps[7][:, 96:128], s_ones[:], s_caQ[0:1, 0:32],
                start=True, stop=True), after=[e_cq], inc="pe")
            t_ws = em.op("vector", lambda e: e.tensor_sub(
                s_wdtT[:], ps[7][:, 96:128], s_caT[:]), after=[t_qb], inc="dv")
            t_we = em.op("scalar", lambda e: e.activation(
                s_wdtT[:], s_wdtT[:], AF.Exp), after=[t_ws], inc="ac")
            t_wd = em.op("vector", lambda e: e.tensor_mul(
                s_wdtT[:], s_wdtT[:], s_dspT[:]), after=[t_we, e_dt], inc="dv")
            t_ec = em.op("tensor", lambda e: e.matmul(
                ps[7][:, 128:160], s_ones[0:1, :], s_caQ[0:1, 32:64],
                start=True, stop=True), after=[t_er, t_ws], inc="pe")
            t_ecv = em.op("vector", lambda e: e.tensor_copy(
                s_Ecol[:], ps[7][:, 128:160]), after=[t_ec], inc="dv")

            if c == 0:
                lexp_tok = [wb3_tok] * 4
                gm_tok = [wb3_tok, wb3_tok]
                yh_tok = [wb3_tok, wb3_tok]
            else:
                lexp_tok = [{}] * 4
                gm_tok = [{}, {}]
                yh_tok = [{}, {}]
            o1_tok = [{}, {}]
            xds_tok = [{}, {}]
            l4_tok = {}
            for g in range(8):
                em.op("tensor", lambda e, g=g: e.matmul(
                    ps[0][:, :], s_ones[:], s_car[0:1, g * 512:(g + 1) * 512],
                    start=True, stop=False),
                    after=[t_cr, merge(*lexp_tok), t_eca])
                l4_tok = em.op("tensor", lambda e: e.matmul(
                    ps[0][:, :], s_identb[:], s_masknegb[:],
                    start=False, stop=True), inc="pe")
                for hh in range(4):
                    h = g * 4 + hh
                    kv = h // 4
                    hb = h % 2
                    base = (h % 2) * 64
                    if kv % 2 == h % 2:
                        Bn = s_xbcc[base:base + 64, pc, 4 + kv // 2, :]
                    else:
                        Bn = s_Bsw[base:base + 64, kv // 2, :]
                    Cn = s_xbcc[base:base + 64, pc, 8 + h // 2, :]
                    lexp_tok[hh] = em.op("scalar", lambda e, hh=hh, h=h:
                                         e.activation(
                        s_Lx[:, hh, :], ps[0][:, hh * 128:(hh + 1) * 128],
                        AF.Exp, bias=s_ncaT[:, h:h + 1]),
                        after=[l4_tok, t_nca, lx_cons[hh]], inc="ac")
                    g_mm = em.op("tensor", lambda e, Bn=Bn, Cn=Cn, hb=hb:
                                 e.matmul(
                        ps[1 + hb][:, 0:128], Bn, Cn, start=True, stop=True),
                        after=[t_x, gm_tok[hb]], inc="pe")
                    gm_tok[hb] = em.op("vector", lambda e, hb=hb, hh=hh, h=h:
                                       e.scalar_tensor_tensor(
                        s_Gm[:, hb, :], ps[1 + hb][:, 0:128],
                        s_dspT[:, h:h + 1], s_Lx[:, hh, :], ALU.mult, ALU.mult),
                        after=[g_mm, lexp_tok[hh], ymm_tok[hb]], inc="dv")
                    lx_cons[hh] = gm_tok[hb]
                    xds_tok[hb] = em.op("vector", lambda e, pc=pc, kv=kv, hb=hb,
                                        h=h: e.tensor_scalar(
                        s_Xds[:, hb, :], s_xT[:, pc, kv * 64:kv * 64 + 64],
                        s_wdtT[:, h:h + 1], None, ALU.mult),
                        after=[t_wd, smm_tok[hb]], inc="dv")
                    pY = ps[3 + hb]
                    pYs = ps[5 + hb]
                    em.op("tensor", lambda e, pc=pc, kv=kv, hb=hb, pY=pY:
                          e.matmul(
                        pY[:, 0:64], s_Gm[:, hb, :],
                        s_xT[:, pc, kv * 64:kv * 64 + 64],
                        start=True, stop=False),
                        after=[gm_tok[hb], yh_tok[hb]])
                    ymm_tok[hb] = em.op("tensor",
                                        lambda e, pc=pc, kv=kv, h=h, pY=pY:
                                        e.matmul(
                        pY[:, 0:64], s_DIh[:, h, :],
                        s_xT[:, pc, kv * 64:kv * 64 + 64],
                        start=False, stop=True), after=[dih_tok], inc="pe")
                    if c > 0:
                        ys_mm = em.op("tensor", lambda e, Cn=Cn, h=h, pYs=pYs,
                                      base=base: e.matmul(
                            pYs[:, 0:64], Cn,
                            s_stb[base:base + 64, h // 2, :],
                            start=True, stop=True),
                            after=[st_bf_tok[h], o1_tok[hb], xbt_tok],
                            inc="pe")
                    else:
                        ys_mm = None
                    s_mm_t = em.op("tensor", lambda e, pc=pc, kv=kv, hb=hb,
                                   base=base: e.matmul(
                        ps[7][base:base + 64, 448:512],
                        s_BT[:, pc, kv * 64:kv * 64 + 64],
                        s_Xds[:, hb, :], start=True, stop=True),
                        after=[xds_tok[hb], e_BT, st_upd_tok[(h - 1) % NH]],
                        inc="pe")
                    smm_tok[hb] = s_mm_t
                    if c == 0:
                        st_upd_tok[h] = em.op("vector", lambda e, h=h, base=base:
                                              e.tensor_copy(
                            s_stf[base:base + 64, h // 2, :],
                            ps[7][base:base + 64, 448:512]),
                            after=[s_mm_t], inc="dv")
                    else:
                        st_upd_tok[h] = em.op("vector", lambda e, h=h, base=base:
                                              e.scalar_tensor_tensor(
                            s_stf[base:base + 64, h // 2, :],
                            s_stf[base:base + 64, h // 2, :],
                            s_Ecol[base:base + 64, h:h + 1],
                            ps[7][base:base + 64, 448:512],
                            ALU.mult, ALU.add), after=[s_mm_t, t_ecv], inc="dv")
                    st_bf_tok[h] = em.op("vector", lambda e, h=h, base=base:
                                         e.tensor_copy(
                        s_stb[base:base + 64, h // 2, :],
                        s_stf[base:base + 64, h // 2, :]),
                        after=[st_upd_tok[h], ys_mm], inc="dv")
                    if c > 0:
                        o1_tok[hb] = em.op("vector", lambda e, hb=hb, h=h,
                                           pYs=pYs: e.tensor_scalar(
                            s_o1[:, hb, :], pYs[:, 0:64], s_ecaT[:, h:h + 1],
                            None, ALU.mult), after=[ys_mm, t_eca], inc="dv")
                        yh_tok[hb] = em.op("vector", lambda e, pc=pc, hb=hb, h=h,
                                           pY=pY: e.tensor_add(
                            s_yt[:, pc, h * 64:(h + 1) * 64], pY[:, 0:64],
                            s_o1[:, hb, :]),
                            after=[ymm_tok[hb], o1_tok[hb], y_sp_tok[pc]],
                            inc="dv")
                    else:
                        yh_tok[hb] = em.op("vector", lambda e, pc=pc, hb=hb, h=h,
                                           pY=pY: e.tensor_copy(
                            s_yt[:, pc, h * 64:(h + 1) * 64], pY[:, 0:64]),
                            after=[ymm_tok[hb], y_sp_tok[pc]], inc="dv")
            ysp_dbg = None
            if debug:
                ysp_dbg = em.dma("gpsimd", lambda e, c=c, pc=pc: e.dma_start(
                    out=d_ysp[c * 128:(c + 1) * 128, :], in_=s_yt[:, pc, :]),
                    after=[yh_tok[0], yh_tok[1]])
            tg3 = em.op("vector", lambda e, pc=pc: e.tensor_mul(
                s_yt[:, pc, :], s_yt[:, pc, :], s_zs3[:, pc, :]),
                after=[yh_tok[0], yh_tok[1], t_z, ysp_dbg], inc="dv")
            tq3 = em.op("scalar", lambda e, pc=pc: e.activation(
                s_zs3[:, pc, :], s_yt[:, pc, :], AF.Square,
                accum_out=s_col3[:, 2 * pc:2 * pc + 1]), after=[tg3], inc="ac")
            tq3 = em.op("scalar", lambda e, pc=pc: e.activation(
                s_col3[:, 2 * pc + 1:2 * pc + 2], s_col3[:, 2 * pc:2 * pc + 1],
                AF.Ln, scale=1.0 / D, bias=s_eps[:, 0:1]), after=[tq3],
                inc="ac")
            tq3 = em.op("scalar", lambda e, pc=pc: e.activation(
                s_col3[:, 2 * pc:2 * pc + 1], s_col3[:, 2 * pc + 1:2 * pc + 2],
                AF.Exp, scale=-0.5), after=[tq3], inc="ac")
            tn3 = em.op("vector", lambda e, pc=pc: e.scalar_tensor_tensor(
                s_ynbf3[:], s_yt[:, pc, :], s_col3[:, 2 * pc:2 * pc + 1],
                s_wbs3[:], ALU.mult, ALU.mult),
                after=[tq3, wb3_tok, ynbf_cons3], inc="dv")
            zs_cons[pc] = tq3
            tspl3 = em.dma("sync", lambda e, c=c: e.dma_start(
                out=d_ynbf[c * 128:(c + 1) * 128, :], in_=s_ynbf3[:]),
                after=[tn3], sem="spA")
            ynbf_cons3 = tspl3
            for j in range(NKD):
                em.dma("sync", lambda e, c=c, j=j: e.dma_start(
                    out=s_slot[:, j, c * 128:(c + 1) * 128],
                    in_=d_ynbf[c * 128:(c + 1) * 128, j * 128:(j + 1) * 128],
                    transpose=True), after=[tspl3] if j == 0 else None,
                    sem="dt")
            y_sp_tok[pc] = tn3
            car_cons = l4_tok
            chunk_tails[c + 2] = merge(
                st_bf_tok[NH - 1], smm_tok[0], smm_tok[1],
                ymm_tok[0], ymm_tok[1], merge(*lexp_tok),
                yh_tok[0], yh_tok[1])
        finish(em)
        close_sb("s_xbcc", "s_car", "s_xT", "s_BT", "s_caT", "s_ncaT", "s_ecaT",
                 "s_dspT", "s_wdtT", "s_caQ", "s_Ecol", "s_Lx", "s_Gm", "s_Xds",
                 "s_o1", "s_yt", "s_stf", "s_stb", "s_Dbc", "s_DIh", "s_drow3",
                 "s_Bsw", "s_zs3", "s_ynbf3", "s_wbs3", "s_col3",
                 "s_wotr", "s_dsp", "s_can", "s_dtf")

        em = new_phase()
        s_wo4 = open_sb("s_wo4", [128, NKD, D], BF16)
        s_wbp = open_sb("s_wbp", [128, D], F32)
        s_row4b = open_sb("s_row4b", [1, D], F32)
        s_x4 = open_sb("s_x4", [128, 2, D], F32)
        s_h14 = open_sb("s_h14", [128, 2, D], F32)
        s_col4b = open_sb("s_col4b", [128, 8], F32)
        s_h1nbf = open_sb("s_h1nbf", [128, D], BF16)

        woload = em.dma("gpsimd", lambda e: e.dma_start(
            out=s_wo4[:], in_=d_wobf.rearrange("k p d -> p k d")))
        tr2 = em.dma("sync", lambda e: e.dma_start(out=s_row4b[:],
                                                   in_=d_postlnw))
        wtoks = []
        for j in range(4):
            tm = em.op("tensor", lambda e, j=j: e.matmul(
                ps[4 + j][:, :], s_ones[:], s_row4b[0:1, j * 512:(j + 1) * 512],
                start=True, stop=True), after=[tr2], inc="pe")
            wtoks.append(em.op("vector", lambda e, j=j: e.tensor_copy(
                s_wbp[:, j * 512:(j + 1) * 512], ps[4 + j][:, :]), after=[tm],
                inc="dv"))
        wbp_tok = merge(*wtoks)

        x_cons = [{}, {}]
        h14_cons = [{}, {}]
        h1n_cons = {}
        bank_cons = [wbp_tok] * 8
        for tb in range(NT):
            phh = tb % 2
            mm_done = [None] * 4
            for k in range(NKD):
                for ob in range(4):
                    bank = phh * 4 + ob
                    mm_done[ob] = em.op(
                        "tensor", lambda e, tb=tb, k=k, ob=ob, bank=bank:
                        e.matmul(
                            ps[bank][:, :], s_slot[:, k, tb * 128:(tb + 1) * 128],
                            s_wo4[:, k, ob * 512:(ob + 1) * 512],
                            start=(k == 0), stop=(k == NKD - 1)),
                        after=[woload, bank_cons[bank]] if k == 0 else None,
                        inc="pe" if k == NKD - 1 else None)
            tx = em.dma("gpsimd", lambda e, tb=tb, phh=phh: e.dma_start(
                out=s_x4[:, phh, :], in_=d_x[tb * 128:(tb + 1) * 128, :]),
                after=[x_cons[phh]])
            evs = []
            for ob in range(4):
                bank = phh * 4 + ob
                ev = em.op("vector", lambda e, phh=phh, ob=ob, bank=bank:
                           e.tensor_add(
                    s_h14[:, phh, ob * 512:(ob + 1) * 512], ps[bank][:, :],
                    s_x4[:, phh, ob * 512:(ob + 1) * 512]),
                    after=[mm_done[ob], tx, h14_cons[phh]], inc="dv")
                bank_cons[bank] = ev
                evs.append(ev)
            evm = merge(*evs)
            tsp = em.dma("gpsimd", lambda e, tb=tb, phh=phh: e.dma_start(
                out=d_h1sp[tb * 128:(tb + 1) * 128, :], in_=s_h14[:, phh, :]),
                after=[evm])
            tqs = em.op("scalar", lambda e, phh=phh: e.activation(
                s_x4[:, phh, :], s_h14[:, phh, :], AF.Square,
                accum_out=s_col4b[:, 4 * phh:4 * phh + 1]), after=[evm],
                inc="ac")
            x_cons[phh] = tqs
            tq = em.op("scalar", lambda e, phh=phh: e.activation(
                s_col4b[:, 4 * phh + 1:4 * phh + 2],
                s_col4b[:, 4 * phh:4 * phh + 1],
                AF.Ln, scale=1.0 / D, bias=s_eps[:, 0:1]), after=[tqs],
                inc="ac")
            tq = em.op("scalar", lambda e, phh=phh: e.activation(
                s_col4b[:, 4 * phh:4 * phh + 1],
                s_col4b[:, 4 * phh + 1:4 * phh + 2],
                AF.Exp, scale=-0.5), after=[tq], inc="ac")
            tn2 = em.op("vector", lambda e, phh=phh: e.scalar_tensor_tensor(
                s_h1nbf[:], s_h14[:, phh, :],
                s_col4b[:, 4 * phh:4 * phh + 1], s_wbp[:],
                ALU.mult, ALU.mult), after=[tq, h1n_cons], inc="dv")
            tspl = em.dma("sync", lambda e, tb=tb: e.dma_start(
                out=d_h1nbf[tb * 128:(tb + 1) * 128, :], in_=s_h1nbf[:]),
                after=[tn2], sem="spA")
            h1n_cons = tspl
            h14_cons[phh] = merge(tsp, tn2)
            for j in range(NKD):
                em.dma("sync", lambda e, tb=tb, j=j: e.dma_start(
                    out=s_slot[:, j, tb * 128:(tb + 1) * 128],
                    in_=d_h1nbf[tb * 128:(tb + 1) * 128,
                                j * 128:(j + 1) * 128],
                    transpose=True),
                    after=[tspl, mm_done[0], mm_done[1], mm_done[2],
                           mm_done[3]] if j == 0 else None, sem="dt")
        finish(em)
        close_sb("s_wo4", "s_wbp", "s_row4b", "s_x4", "s_h14", "s_col4b",
                 "s_h1nbf")

        em = new_phase()
        if debug:
            for j in range(NKD):
                em.dma("sync", lambda e, j=j: e.dma_start(
                    out=d_slotdbg[j, :, :], in_=s_slot[:, j, :]))
        s_wg5 = open_sb("s_wg5", [128, 2, NKD, 128], BF16)
        s_wu5 = open_sb("s_wu5", [128, 2, NKD, 128], BF16)
        s_sil5 = open_sb("s_sil5", [128, 4, 512], F32)
        s_m5 = open_sb("s_m5", [128, 4, 512], BF16)
        s_wdtr = open_sb("s_wdtr", [128, 2, D], BF16)

        wd_ld = [{}, {}]
        for j in range(DFF // 128):
            pj = j % 2
            tl = em.dma("gpsimd", lambda e, j=j, pj=pj: e.dma_start(
                out=s_wdtr[:, pj, :], in_=d_wd[j * 128:(j + 1) * 128, :]),
                after=[wd_ld[pj]])
            wd_ld[pj] = em.dma("sync", lambda e, j=j, pj=pj: e.dma_start(
                out=d_wdbf[j, :, :], in_=s_wdtr[:, pj, :]), after=[tl])
        wdbf_done = merge(*wd_ld)

        wlast5 = [{}, {}]
        ev_cons = [{}] * 4
        bank5 = [{}] * 8
        for dc in range(DFF // 128):
            pw = dc % 2
            twg = em.dma("gpsimd", lambda e, dc=dc, pw=pw: e.dma_start(
                out=s_wg5[:, pw, :, :],
                in_=d_wg[:, dc * 128:(dc + 1) * 128].rearrange(
                    "(kc p) c -> p kc c", p=128)), after=[wlast5[pw]])
            twu = em.dma("gpsimd", lambda e, dc=dc, pw=pw: e.dma_start(
                out=s_wu5[:, pw, :, :],
                in_=d_wu[:, dc * 128:(dc + 1) * 128].rearrange(
                    "(kc p) c -> p kc c", p=128)), after=[twg])
            mmh = [None] * 4
            mmg = [None] * 4
            for k in range(NKD):
                for tbq in range(4):
                    mmh[tbq] = em.op("tensor", lambda e, pw=pw, k=k, tbq=tbq:
                                     e.matmul(
                        ps[tbq][:, :], s_wg5[:, pw, k, :],
                        s_slot[:, k, tbq * 512:(tbq + 1) * 512],
                        start=(k == 0), stop=(k == NKD - 1)),
                        after=[twg, bank5[tbq]] if k == 0 else None,
                        inc="pe" if k == NKD - 1 else None)
            for k in range(NKD):
                for tbq in range(4):
                    mmg[tbq] = em.op("tensor", lambda e, pw=pw, k=k, tbq=tbq:
                                     e.matmul(
                        ps[4 + tbq][:, :], s_wu5[:, pw, k, :],
                        s_slot[:, k, tbq * 512:(tbq + 1) * 512],
                        start=(k == 0), stop=(k == NKD - 1)),
                        after=[twu, bank5[4 + tbq]] if k == 0 else None,
                        inc="pe" if k == NKD - 1 else None)
            wlast5[pw] = merge(mmh[3], mmg[3])
            for tbq in range(4):
                t1 = em.op("scalar", lambda e, tbq=tbq: e.activation(
                    s_sil5[:, tbq, :], ps[tbq][:, :], AF.Silu),
                    after=[mmh[tbq], ev_cons[tbq]], inc="ac")
                bank5[tbq] = t1
                t2 = em.op("vector", lambda e, tbq=tbq: e.tensor_mul(
                    s_m5[:, tbq, :], s_sil5[:, tbq, :], ps[4 + tbq][:, :]),
                    after=[t1, mmg[tbq], ev_cons[tbq]], inc="dv")
                bank5[4 + tbq] = t2
                ev_cons[tbq] = em.dma("sync", lambda e, dc=dc, tbq=tbq:
                                      e.dma_start(
                    out=d_msp[dc, :, tbq * 512:(tbq + 1) * 512],
                    in_=s_m5[:, tbq, :]), after=[t2])
        finish(em)
        close_sb("s_wg5", "s_wu5", "s_sil5", "s_m5", "s_wdtr")

        em = new_phase()
        NKB = DFF // 128
        KBG = 8
        s_wd6 = open_sb("s_wd6", [128, 2, KBG, D], BF16)
        s_m6 = open_sb("s_m6", [128, 2, KBG, 128], BF16)
        s_acc = open_sb("s_acc", [128, 4, D], F32)
        s_h16 = open_sb("s_h16", [128, 2, D], F32)
        s_x6 = open_sb("s_x6", [128, D], F32)
        s_col6 = open_sb("s_col6", [128, 8], F32)
        s_x2bf = open_sb("s_x2bf", [128, 2, D], BF16)

        wd_cons = [wdbf_done, wdbf_done]
        wdmm = [{}, {}]
        m_free = [{}] * 2
        acc_cons = [{}] * 4
        bank6 = [{}] * 8
        tail_cons = [{}, {}]
        x2T_cons = [{}, {}]
        x6_cons = {}
        for tg in range(4):
            for kbg in range(NKB // KBG):
                pw = kbg % 2
                wld = em.dma("scalar", lambda e, kbg=kbg, pw=pw: e.dma_start(
                    out=s_wd6[:, pw, :, :],
                    in_=d_wdbf[kbg * KBG:(kbg + 1) * KBG, :, :].rearrange(
                        "kb p d -> p kb d")),
                    after=[wd_cons[pw], wdmm[pw]])
                wd_cons[pw] = wld
                for ti in range(4):
                    t = tg * 4 + ti
                    mp = ti % 2
                    mld = em.dma("gpsimd", lambda e, kbg=kbg, t=t, mp=mp:
                                 e.dma_start(
                        out=s_m6[:, mp, :, :],
                        in_=d_msp[kbg * KBG:(kbg + 1) * KBG, :,
                                  t * 128:(t + 1) * 128].rearrange(
                            "kb p t -> p kb t")),
                        after=[m_free[mp]])
                    last_kb_tok = None
                    for kb in range(KBG):
                        for ob in range(4):
                            bank = (ti % 2) * 4 + ob
                            tokmm = em.op(
                                "tensor",
                                lambda e, mp=mp, pw=pw, kb=kb, ob=ob, bank=bank:
                                e.matmul(
                                    ps[bank][:, :], s_m6[:, mp, kb, :],
                                    s_wd6[:, pw, kb, ob * 512:(ob + 1) * 512],
                                    start=(kb == 0), stop=(kb == KBG - 1)),
                                after=[mld, wld, bank6[bank]]
                                if kb == 0 else None,
                                inc="pe" if (kb == KBG - 1 and ob == 3)
                                else None)
                        last_kb_tok = tokmm
                    m_free[mp] = tokmm
                    wdmm[pw] = last_kb_tok
                    for ob in range(4):
                        bank = (ti % 2) * 4 + ob
                        if kbg == 0:
                            bank6[bank] = em.op(
                                "vector", lambda e, ti=ti, ob=ob, bank=bank:
                                e.tensor_copy(
                                    s_acc[:, ti, ob * 512:(ob + 1) * 512],
                                    ps[bank][:, :]),
                                after=[last_kb_tok, acc_cons[ti]], inc="dv")
                        else:
                            bank6[bank] = em.op(
                                "vector", lambda e, ti=ti, ob=ob, bank=bank:
                                e.tensor_add(
                                    s_acc[:, ti, ob * 512:(ob + 1) * 512],
                                    s_acc[:, ti, ob * 512:(ob + 1) * 512],
                                    ps[bank][:, :]),
                                after=[last_kb_tok], inc="dv")
            for ti in range(4):
                t = tg * 4 + ti
                p = t % 2
                acc_done = merge(*[bank6[(ti % 2) * 4 + ob] for ob in range(4)])
                th1 = em.dma("sync", lambda e, t=t, p=p: e.dma_start(
                    out=s_h16[:, p, :], in_=d_h1sp[t * 128:(t + 1) * 128, :]),
                    after=[tail_cons[p]])
                tx = em.dma("gpsimd", lambda e, t=t: e.dma_start(
                    out=s_x6[:], in_=d_x[t * 128:(t + 1) * 128, :]),
                    after=[x6_cons])
                t1 = em.op("vector", lambda e, ti=ti, p=p: e.tensor_add(
                    s_acc[:, ti, :], s_acc[:, ti, :], s_h16[:, p, :]),
                    after=[acc_done, th1], inc="dv")
                t1 = em.op("vector", lambda e, ti=ti: e.tensor_add(
                    s_acc[:, ti, :], s_acc[:, ti, :], s_x6[:]),
                    after=[t1, tx], inc="dv")
                ts_ = em.op("vector", lambda e, ti=ti, p=p: e.reduce_sum(
                    s_col6[:, 4 * p:4 * p + 1], s_acc[:, ti, :], axis=AX.X),
                    after=[t1], inc="dv")
                tqs = em.op("scalar", lambda e, ti=ti, p=p: e.activation(
                    s_x6[:], s_acc[:, ti, :], AF.Square,
                    accum_out=s_col6[:, 4 * p + 1:4 * p + 2]), after=[t1],
                    inc="ac")
                x6_cons = merge(t1, tqs)
                tm = em.op("vector", lambda e, p=p: e.tensor_scalar(
                    s_col6[:, 4 * p:4 * p + 1], s_col6[:, 4 * p:4 * p + 1],
                    1.0 / D, None, ALU.mult), after=[ts_], inc="dv")
                tv = em.op("vector", lambda e, p=p: e.tensor_scalar(
                    s_col6[:, 4 * p + 1:4 * p + 2],
                    s_col6[:, 4 * p + 1:4 * p + 2],
                    1.0 / D, None, ALU.mult), after=[tqs], inc="dv")
                tv = em.op("vector", lambda e, p=p: e.scalar_tensor_tensor(
                    s_col6[:, 4 * p + 2:4 * p + 3], s_col6[:, 4 * p:4 * p + 1],
                    s_col6[:, 4 * p:4 * p + 1], s_col6[:, 4 * p + 1:4 * p + 2],
                    ALU.mult, ALU.subtract), after=[tm, tv], inc="dv")
                tv = em.op("vector", lambda e, p=p: e.tensor_scalar(
                    s_col6[:, 4 * p + 2:4 * p + 3],
                    s_col6[:, 4 * p + 2:4 * p + 3],
                    -1.0, None, ALU.mult), after=[tv], inc="dv")
                tq2 = em.op("scalar", lambda e, p=p: e.activation(
                    s_col6[:, 4 * p + 3:4 * p + 4],
                    s_col6[:, 4 * p + 2:4 * p + 3],
                    AF.Ln, bias=s_eps[:, 0:1]), after=[tv], inc="ac")
                tq2 = em.op("scalar", lambda e, p=p: e.activation(
                    s_col6[:, 4 * p + 3:4 * p + 4],
                    s_col6[:, 4 * p + 3:4 * p + 4],
                    AF.Exp, scale=-0.5), after=[tq2], inc="ac")
                tx2 = em.op("vector", lambda e, ti=ti, p=p: e.tensor_scalar(
                    s_acc[:, ti, :], s_acc[:, ti, :], s_col6[:, 4 * p:4 * p + 1],
                    s_col6[:, 4 * p + 3:4 * p + 4], ALU.subtract, ALU.mult),
                    after=[tq2], inc="dv")
                tsp = em.dma("sync", lambda e, t=t, ti=ti: e.dma_start(
                    out=d_x2sp[t * 128:(t + 1) * 128, :], in_=s_acc[:, ti, :]),
                    after=[tx2])
                tbf = em.op("vector", lambda e, ti=ti, p=p: e.tensor_copy(
                    s_x2bf[:, p, :], s_acc[:, ti, :]),
                    after=[tx2, x2T_cons[p]], inc="dv")
                tspl2 = em.dma("sync", lambda e, t=t, p=p: e.dma_start(
                    out=d_x2bf[t * 128:(t + 1) * 128, :], in_=s_x2bf[:, p, :]),
                    after=[tbf], sem="spA" if t % 2 == 0 else "spB")
                x2T_cons[p] = tspl2
                for j in range(NKD):
                    em.dma("sync", lambda e, t=t, j=j: e.dma_start(
                        out=s_slot[:, j, t * 128:(t + 1) * 128],
                        in_=d_x2bf[t * 128:(t + 1) * 128,
                                   j * 128:(j + 1) * 128],
                        transpose=True), after=[tspl2] if j == 0 else None,
                        sem="dt")
                acc_cons[ti] = merge(tsp, tbf)
                tail_cons[p] = merge(tsp, tbf, tqs)
        finish(em)
        close_sb("s_wd6", "s_m6", "s_acc", "s_h16", "s_x6", "s_col6", "s_x2bf")

        em = new_phase()
        s_gw7s = open_sb("s_gw7s", [128, NKD, 8], BF16)
        s_e7 = open_sb("s_e7", [128, 4, 8], F32)
        s_c7 = open_sb("s_c7", [128, 8], F32)
        tw = em.dma("gpsimd", lambda e: e.dma_start(
            out=s_gw7s[:], in_=d_gw.rearrange("(kc p) c -> p kc c", p=128)))
        em.op("vector", lambda e: e.memset(s_wg7[:, :, 8:10], 1.0), inc="dv")
        bank7 = [{}, {}]
        for tb in range(NT):
            pb = tb % 2
            mm = None
            for k in range(NKD):
                mm = em.op("tensor", lambda e, tb=tb, k=k, pb=pb: e.matmul(
                    ps[pb][:, 0:8], s_slot[:, k, tb * 128:(tb + 1) * 128],
                    s_gw7s[:, k, :], start=(k == 0), stop=(k == NKD - 1)),
                    after=[tw, bank7[pb]] if k == 0 else None,
                    inc="pe" if k == NKD - 1 else None)
            t1 = em.op("vector", lambda e, pb=pb: e.reduce_max(
                s_c7[:, 0:1], ps[pb][:, 0:8], axis=AX.X), after=[mm], inc="dv")
            t1 = em.op("vector", lambda e: e.tensor_scalar(
                s_c7[:, 1:2], s_c7[:, 0:1], -1.0, None, ALU.mult), after=[t1],
                inc="dv")
            t2 = em.op("scalar", lambda e, pb=pb: e.activation(
                s_e7[:, 0, :], ps[pb][:, 0:8], AF.Exp, bias=s_c7[:, 1:2]),
                after=[t1], inc="ac")
            bank7[pb] = t2
            t3 = em.op("vector", lambda e: e.tensor_scalar(
                s_e7[:, 1, :], s_e7[:, 0, :], 1.0, None, ALU.is_ge),
                after=[t2], inc="dv")
            t3 = em.op("vector", lambda e: e.tensor_mul(
                s_e7[:, 2, :], s_e7[:, 0, :], s_e7[:, 1, :]), after=[t3],
                inc="dv")
            t3 = em.op("vector", lambda e: e.tensor_sub(
                s_e7[:, 2, :], s_e7[:, 0, :], s_e7[:, 2, :]), after=[t3],
                inc="dv")
            t3 = em.op("vector", lambda e: e.reduce_max(
                s_c7[:, 2:3], s_e7[:, 2, :], axis=AX.X), after=[t3], inc="dv")
            t4 = em.op("vector", lambda e: e.tensor_scalar(
                s_e7[:, 3, :], s_e7[:, 0, :], s_c7[:, 2:3], None, ALU.is_ge),
                after=[t3], inc="dv")
            t4 = em.op("vector", lambda e: e.tensor_mul(
                s_e7[:, 3, :], s_e7[:, 3, :], s_e7[:, 0, :]), after=[t4],
                inc="dv")
            t5 = em.op("vector", lambda e: e.reduce_sum(
                s_c7[:, 3:4], s_e7[:, 3, :], axis=AX.X), after=[t4], inc="dv")
            t5 = em.op("vector", lambda e: e.reciprocal(
                s_c7[:, 4:5], s_c7[:, 3:4]), after=[t5], inc="dv")
            em.op("vector", lambda e, tb=tb: e.tensor_scalar(
                s_wg7[:, tb, 0:8], s_e7[:, 3, :], s_c7[:, 4:5], None, ALU.mult),
                after=[t5], inc="dv")
        finish(em)
        close_sb("s_gw7s", "s_e7", "s_c7")

        em = new_phase()
        NDC = 11
        s_w13 = open_sb("s_w13", [128, 2, 2, NKD, 128], BF16)
        s_w28 = open_sb("s_w28", [128, NDC, D], BF16)
        s_sil8 = open_sb("s_sil8", [128, 2, 512], F32)
        s_act8 = open_sb("s_act8", [128, NDC, L], BF16)
        s_yb8 = open_sb("s_yb8", [128, 2, D], F32)
        s_x2f = open_sb("s_x2f", [128, D], F32)
        s_c8 = open_sb("s_c8", [128, 8], F32)

        w13_last = [{}, {}]
        w28_last = [{}, {}]
        act_cons = {}
        yb_cons = [{}, {}]
        bank8 = [{}] * 8
        sil_cons = [{}, {}]

        experts = [("e", i) for i in range(NE)] + [("s", 0), ("s", 1)]
        out_toks8 = []
        x2f_cons = {}
        for ei, (kind, e_id) in enumerate(experts):
            pe8 = ei % 2
            if kind == "e":
                w2src = d_w2[e_id].rearrange("(kb p) d -> p kb d", p=128)
            else:
                w2src = d_sw2[e_id * DI:(e_id + 1) * DI, :].rearrange(
                    "(kb p) d -> p kb d", p=128)
            w28_ld = em.dma("gpsimd", lambda e, w2src=w2src:
                            e.dma_start(out=s_w28[:], in_=w2src),
                            after=[w28_last[0]])
            mmh = [None] * 4
            mmg = [None] * 4
            for dc in range(NDC):
                pw = dc % 2
                if kind == "e":
                    src1 = d_w1[e_id, :, dc * 128:(dc + 1) * 128]
                    src3 = d_w3[e_id, :, dc * 128:(dc + 1) * 128]
                else:
                    off = e_id * DI
                    src1 = d_sw1[:, off + dc * 128:off + (dc + 1) * 128]
                    src3 = d_sw3[:, off + dc * 128:off + (dc + 1) * 128]
                t1 = em.dma("gpsimd", lambda e, src1=src1, pw=pw: e.dma_start(
                    out=s_w13[:, pw, 0, :, :],
                    in_=src1.rearrange("(kc p) c -> p kc c", p=128)),
                    after=[w13_last[pw]])
                t3 = em.dma("gpsimd", lambda e, src3=src3, pw=pw: e.dma_start(
                    out=s_w13[:, pw, 1, :, :],
                    in_=src3.rearrange("(kc p) c -> p kc c", p=128)),
                    after=[t1])
                for k in range(NKD):
                    for tbq in range(4):
                        mmh[tbq] = em.op("tensor", lambda e, pw=pw, k=k, tbq=tbq:
                                         e.matmul(
                            ps[tbq][:, :], s_w13[:, pw, 0, k, :],
                            s_slot[:, k, tbq * 512:(tbq + 1) * 512],
                            start=(k == 0), stop=(k == NKD - 1)),
                            after=[t1, bank8[tbq]] if k == 0 else None,
                            inc="pe" if k == NKD - 1 else None)
                for k in range(NKD):
                    for tbq in range(4):
                        mmg[tbq] = em.op("tensor", lambda e, pw=pw, k=k, tbq=tbq:
                                         e.matmul(
                            ps[4 + tbq][:, :], s_w13[:, pw, 1, k, :],
                            s_slot[:, k, tbq * 512:(tbq + 1) * 512],
                            start=(k == 0), stop=(k == NKD - 1)),
                            after=[t3, bank8[4 + tbq]] if k == 0 else None,
                            inc="pe" if k == NKD - 1 else None)
                w13_last[pw] = merge(mmh[3], mmg[3])
                for tbq in range(4):
                    sp8 = tbq % 2
                    tsl = em.op("scalar", lambda e, tbq=tbq, sp8=sp8:
                                e.activation(
                        s_sil8[:, sp8, :], ps[tbq][:, :], AF.Silu),
                        after=[mmh[tbq], sil_cons[sp8]], inc="ac")
                    bank8[tbq] = tsl
                    tmul = em.op("vector", lambda e, dc=dc, tbq=tbq, sp8=sp8:
                                 e.tensor_mul(
                        s_act8[:, dc, tbq * 512:(tbq + 1) * 512],
                        s_sil8[:, sp8, :],
                        ps[4 + tbq][:, :]), after=[tsl, mmg[tbq], act_cons],
                        inc="dv")
                    bank8[4 + tbq] = tmul
                    sil_cons[sp8] = tmul
            act_done = merge(*bank8)
            new_act_cons = []
            w2_last_mm = None
            for t in range(NT):
                pb = t % 2
                mm2 = [None] * 4
                for kb in range(NDC):
                    for ob in range(4):
                        bank = pb * 4 + ob
                        mm2[ob] = em.op(
                            "tensor",
                            lambda e, t=t, kb=kb, ob=ob, pe8=pe8, bank=bank:
                            e.matmul(
                                ps[bank][:, :],
                                s_act8[:, kb, t * 128:(t + 1) * 128],
                                s_w28[:, kb, ob * 512:(ob + 1) * 512],
                                start=(kb == 0), stop=(kb == NDC - 1)),
                            after=[act_done, w28_ld, bank8[bank]]
                            if kb == 0 else None,
                            inc="pe" if kb == NDC - 1 else None)
                w2_last_mm = merge(*mm2)
                if ei < len(experts) - 1:
                    ev = []
                    for ob in range(4):
                        bank = pb * 4 + ob
                        gcol = 8 + e_id if kind == "s" else e_id
                        tokv = em.op("vector", lambda e, t=t, ob=ob, bank=bank,
                                     gcol=gcol, pb=pb: e.tensor_scalar(
                            s_yb8[:, pb, ob * 512:(ob + 1) * 512],
                            ps[bank][:, :],
                            s_wg7[:, t, gcol:gcol + 1], None, ALU.mult),
                            after=[mm2[ob], yb_cons[pb]], inc="dv")
                        bank8[bank] = tokv
                        ev.append(tokv)
                    accop = ALU.bypass if ei == 0 else ALU.add
                    yb_cons[pb] = em.dma("gpsimd",
                                         lambda e, t=t, pb=pb, accop=accop:
                                         e.dma_start(
                        out=d_ymoe[t * 128:(t + 1) * 128, :],
                        in_=s_yb8[:, pb, :],
                        accum_op=accop), after=[merge(*ev)])
                    new_act_cons.append(yb_cons[pb])
                else:
                    tld = em.dma("gpsimd", lambda e, t=t, pb=pb: e.dma_start(
                        out=s_yb8[:, pb, :],
                        in_=d_ymoe[t * 128:(t + 1) * 128, :]),
                        after=[yb_cons[pb]])
                    tld2 = em.dma("gpsimd", lambda e, t=t: e.dma_start(
                        out=s_x2f[:],
                        in_=d_x2sp[t * 128:(t + 1) * 128, :]),
                        after=[yb_cons[pb], x2f_cons])
                    t1 = em.op("vector", lambda e, pb=pb: e.tensor_add(
                        s_yb8[:, pb, :], s_yb8[:, pb, :], s_x2f[:]),
                        after=[tld, tld2], inc="dv")
                    evs9 = []
                    for ob in range(4):
                        bank = pb * 4 + ob
                        tokv = em.op("vector", lambda e, pb=pb, ob=ob,
                                     bank=bank: e.tensor_add(
                            s_x2f[:, ob * 512:(ob + 1) * 512],
                            ps[bank][:, :],
                            s_yb8[:, pb, ob * 512:(ob + 1) * 512]),
                            after=[mm2[ob], t1], inc="dv")
                        bank8[bank] = tokv
                        evs9.append(tokv)
                    e9 = merge(*evs9)
                    ts9 = em.op("vector", lambda e, pb=pb: e.reduce_sum(
                        s_c8[:, 4 * pb:4 * pb + 1], s_x2f[:],
                        axis=AX.X), after=[e9], inc="dv")
                    tq9 = em.op("scalar", lambda e, pb=pb: e.activation(
                        s_yb8[:, pb, :], s_x2f[:], AF.Square,
                        accum_out=s_c8[:, 4 * pb + 1:4 * pb + 2]),
                        after=[e9], inc="ac")
                    tm9 = em.op("vector", lambda e, pb=pb: e.tensor_scalar(
                        s_c8[:, 4 * pb:4 * pb + 1], s_c8[:, 4 * pb:4 * pb + 1],
                        1.0 / D, None, ALU.mult), after=[ts9], inc="dv")
                    tv9 = em.op("vector", lambda e, pb=pb: e.tensor_scalar(
                        s_c8[:, 4 * pb + 1:4 * pb + 2],
                        s_c8[:, 4 * pb + 1:4 * pb + 2],
                        1.0 / D, None, ALU.mult), after=[tq9], inc="dv")
                    tv9 = em.op("vector", lambda e, pb=pb:
                                e.scalar_tensor_tensor(
                        s_c8[:, 4 * pb + 2:4 * pb + 3],
                        s_c8[:, 4 * pb:4 * pb + 1],
                        s_c8[:, 4 * pb:4 * pb + 1],
                        s_c8[:, 4 * pb + 1:4 * pb + 2],
                        ALU.mult, ALU.subtract), after=[tm9, tv9], inc="dv")
                    tv9 = em.op("vector", lambda e, pb=pb: e.tensor_scalar(
                        s_c8[:, 4 * pb + 2:4 * pb + 3],
                        s_c8[:, 4 * pb + 2:4 * pb + 3],
                        -1.0, None, ALU.mult), after=[tv9], inc="dv")
                    tq9b = em.op("scalar", lambda e, pb=pb: e.activation(
                        s_c8[:, 4 * pb + 3:4 * pb + 4],
                        s_c8[:, 4 * pb + 2:4 * pb + 3],
                        AF.Ln, bias=s_eps[:, 0:1]), after=[tv9], inc="ac")
                    tq9b = em.op("scalar", lambda e, pb=pb: e.activation(
                        s_c8[:, 4 * pb + 3:4 * pb + 4],
                        s_c8[:, 4 * pb + 3:4 * pb + 4],
                        AF.Exp, scale=-0.5), after=[tq9b], inc="ac")
                    to9 = em.op("vector", lambda e, pb=pb: e.tensor_scalar(
                        s_x2f[:], s_x2f[:],
                        s_c8[:, 4 * pb:4 * pb + 1],
                        s_c8[:, 4 * pb + 3:4 * pb + 4],
                        ALU.subtract, ALU.mult), after=[tq9b], inc="dv")
                    tst9 = em.dma("sync", lambda e, t=t, pb=pb: e.dma_start(
                        out=d_out[t * 128:(t + 1) * 128, :],
                        in_=s_x2f[:]), after=[to9])
                    yb_cons[pb] = merge(tst9, to9, tq9)
                    x2f_cons = merge(tst9, to9)
                    new_act_cons.append(yb_cons[pb])
                    out_toks8.append(tst9)
            w28_last[0] = w2_last_mm
            act_cons = merge(*new_act_cons)
        em.op("gpsimd", None, after=[merge(*out_toks8)])
        finish(em)
        close_sb("s_w13", "s_w28", "s_sil8", "s_act8", "s_yb8", "s_x2f",
                 "s_c8")

    return nc



def host_inputs(inputs, core):
    f = np.ascontiguousarray
    x = np.asarray(inputs["L0_attn_hidden_states"])
    wip = np.asarray(inputs["L0_attn_in_proj_w"])
    maskneg = np.zeros((128, 512), np.float32)
    s = np.arange(128)[:, None]
    t = np.arange(128)[None, :]
    for j in range(4):
        maskneg[:, j * 128:(j + 1) * 128] = np.where(s > t, MASKNEG, 0.0)
    return {
        "d_x": f(x[core].astype(np.float32)),
        "d_wz": f(wip[0:2048, :].T.astype(np.float32)),
        "d_wxbc": f(wip[2048:5152, :].T.astype(np.float32)),
        "d_convw": f(np.asarray(inputs["L0_attn_conv_w"]).reshape(24, 128, 4)
                     .astype(np.float32)),
        "d_convb": f(np.asarray(inputs["L0_attn_conv_b"]).reshape(24, 128)
                     .T.astype(np.float32)),
        "d_dtb": f(np.asarray(inputs["L0_attn_dt_bias"]).reshape(32, 1)
                   .astype(np.float32)),
        "d_alog": f(np.asarray(inputs["L0_attn_A_log"]).reshape(32, 1)
                    .astype(np.float32)),
        "d_drow": f(np.asarray(inputs["L0_attn_D"]).reshape(1, 32)
                    .astype(np.float32)),
        "d_ssmw": f(np.asarray(inputs["L0_attn_ssm_norm_w"]).reshape(1, D)
                    .astype(np.float32)),
        "d_inlnw": f(np.asarray(inputs["L0_attn_input_ln_w"]).reshape(1, D)
                     .astype(np.float32)),
        "d_postlnw": f(np.asarray(inputs["L0_attn_post_ln_w"]).reshape(1, D)
                       .astype(np.float32)),
        "d_wo": f(np.asarray(inputs["L0_attn_out_proj_w"]).T.astype(np.float32)),
        "d_wg": f(np.asarray(inputs["L0_attn_gate_w"]).T.astype(np.float32)),
        "d_wu": f(np.asarray(inputs["L0_attn_up_w"]).T.astype(np.float32)),
        "d_wd": f(np.asarray(inputs["L0_attn_down_w"]).T.astype(np.float32)),
        "d_gw": f(np.asarray(inputs["L0_ffn_gate_w"]).T.astype(np.float32)),
        "d_w1": f(np.asarray(inputs["L0_ffn_w1"]).astype(np.float32)),
        "d_w3": f(np.asarray(inputs["L0_ffn_w3"]).astype(np.float32)),
        "d_w2": f(np.asarray(inputs["L0_ffn_w2"]).astype(np.float32)),
        "d_sw1": f(np.asarray(inputs["L0_ffn_sw1"]).astype(np.float32)),
        "d_sw3": f(np.asarray(inputs["L0_ffn_sw3"]).astype(np.float32)),
        "d_sw2": f(np.asarray(inputs["L0_ffn_sw2"]).astype(np.float32)),
        "d_ident": np.eye(128, dtype=np.float32),
        "d_maskneg": maskneg,
        "d_ones": np.ones((1, 128), np.float32),
    }


def kernel(**inputs):
    nc = build_program(debug=DEBUG)
    in_maps = [host_inputs(inputs, core) for core in range(8)]
    res = run_bass_kernel_spmd(nc, in_maps, core_ids=list(range(8)))
    return np.stack([np.asarray(r["d_out"], np.float32) for r in res.results],
                    axis=0)
